# version 25
# baseline (speedup 1.0000x reference)
import contextlib
import numpy as np
import ml_dtypes
import concourse.bass as bass
import concourse.mybir as mybir
from concourse.alu_op_type import AluOpType as OP
from concourse.bass_utils import run_bass_kernel_spmd

F32 = mybir.dt.float32
BF16 = mybir.dt.bfloat16
I16 = mybir.dt.int16
U16 = mybir.dt.uint16
ACT = mybir.ActivationFunctionType
AX = mybir.AxisListType.X
BF = ml_dtypes.bfloat16

T = 2048
D = 1024
H = 8
DH = 64
CD = 512
FF = 3072
NC = 8
CM = 12582912.0
R2 = 0.7071067811865476
SQ2P1 = 1.0 + R2
EPS = 1e-5
NT = T // 128
NFC = D // 128
NOC = FF // 128
HWID = H * 65
DEBUG = False
PIPE_ATT = True
STOP_AFTER = 99


class Ctr:
    def __init__(self):
        self.v = 0

    def add(self, n):
        self.v += n
        return self.v


def build_nc():
    nc = bass.Bass("TRN2", target_bir_lowering=False)

    d_x_tm = nc.dram_tensor("x_tm", [T, D], F32, kind="ExternalInput").ap()
    d_x_fT = nc.dram_tensor("x_fT", [D, T], F32, kind="ExternalInput").ap()
    d_wqk = nc.dram_tensor("wqk_r", [D, D], F32, kind="ExternalInput").ap()
    d_bqk = nc.dram_tensor("bqk_r", [D], F32, kind="ExternalInput").ap()
    d_wv = nc.dram_tensor("wv", [D, CD], F32, kind="ExternalInput").ap()
    d_bv = nc.dram_tensor("bv", [CD], F32, kind="ExternalInput").ap()
    d_wout = nc.dram_tensor("wout", [CD, D], F32, kind="ExternalInput").ap()
    d_bout = nc.dram_tensor("bout", [D], F32, kind="ExternalInput").ap()
    d_g1 = nc.dram_tensor("g1", [D], F32, kind="ExternalInput").ap()
    d_be1 = nc.dram_tensor("be1", [D], F32, kind="ExternalInput").ap()
    d_wup = nc.dram_tensor("wup_sl", [FF // NC, D], F32, kind="ExternalInput").ap()
    d_wgt = nc.dram_tensor("wgt_sl", [FF // NC, D], F32, kind="ExternalInput").ap()
    d_wdn = nc.dram_tensor("wdn_sl", [D, FF // NC], F32, kind="ExternalInput").ap()
    d_y = nc.dram_tensor("y_out", [T, D], F32, kind="ExternalOutput").ap()

    d_agu_in = nc.dram_tensor("agu_in", [D, FF // NC], BF16)
    d_agu_out = nc.dram_tensor("agu_out", [NC, D, FF // NC], BF16)
    d_agg_in = nc.dram_tensor("agg_in", [D, FF // NC], BF16)
    d_agg_out = nc.dram_tensor("agg_out", [NC, D, FF // NC], BF16)
    d_agd_in = nc.dram_tensor("agd_in", [FF // NC, D], BF16)
    d_agd_out = nc.dram_tensor("agd_out", [NC, FF // NC, D], BF16)
    d_upraw = nc.dram_tensor("upraw", [NOC, 128, T], F32)
    d_gtraw = nc.dram_tensor("gtraw", [NOC, 128, T], F32)
    d_pp = nc.dram_tensor("ppd", [NOC, 128, T], F32)
    d_zz = nc.dram_tensor("zzd", [NOC, 128, T], F32)
    if DEBUG:
        dbg = {
            "qkT": nc.dram_tensor("dbg_qkT", [128, H * T], BF16,
                                  kind="ExternalOutput").ap(),
            "vaug": nc.dram_tensor("dbg_vaug", [128, NT * HWID], BF16,
                                   kind="ExternalOutput").ap(),
            "row": nc.dram_tensor("dbg_row", [1, 32], F32,
                                  kind="ExternalOutput").ap(),
            "on": nc.dram_tensor("dbg_on", [128, NT * 512], F32,
                                 kind="ExternalOutput").ap(),
            "x2": nc.dram_tensor("dbg_x2", [128, NT * D], F32,
                                 kind="ExternalOutput").ap(),
            "xq": nc.dram_tensor("dbg_xq", [128, NFC * T], BF16,
                                 kind="ExternalOutput").ap(),
            "up": nc.dram_tensor("dbg_up", [NOC, 128, T], F32,
                                 kind="ExternalOutput").ap(),
            "gt": nc.dram_tensor("dbg_gt", [NOC, 128, T], F32,
                                 kind="ExternalOutput").ap(),
            "pp": nc.dram_tensor("dbg_pp", [NOC, 128, T], F32,
                                 kind="ExternalOutput").ap(),
            "zz": nc.dram_tensor("dbg_zz", [NOC, 128, T], F32,
                                 kind="ExternalOutput").ap(),
            "scal": nc.dram_tensor("dbg_scal", [32], F32,
                                   kind="ExternalOutput").ap(),
            "wqkbf": nc.dram_tensor("dbg_wqkbf", [128, NFC * D], BF16,
                                    kind="ExternalOutput").ap(),
            "xfbf": nc.dram_tensor("dbg_xfbf", [128, NFC * T], BF16,
                                   kind="ExternalOutput").ap(),
        }
    d_arin = nc.dram_tensor("arin", [1, 8], F32)
    d_arout = nc.dram_tensor("arout", [1, 8], F32)
    d_scal = nc.dram_tensor("scald", [32], F32)

    outer = contextlib.ExitStack()

    def sbuf_in(stack, name, shape, dtype=F32):
        return stack.enter_context(nc.sbuf_tensor(name, shape, dtype))

    ld = outer.enter_context(nc.semaphore("ld"))
    gd = outer.enter_context(nc.semaphore("gd"))
    ccs = outer.enter_context(nc.semaphore("ccs"))
    te = outer.enter_context(nc.semaphore("te"))
    ve = outer.enter_context(nc.semaphore("ve"))
    se = outer.enter_context(nc.semaphore("se"))
    gc2 = outer.enter_context(nc.semaphore("gc"))
    SEM = {"ld": ld, "gd": gd, "cc": ccs, "te": te, "ve": ve, "se": se,
           "gc": gc2}
    LANES = ["A", "W0", "W1", "U0", "U1", "P0", "P1", "G"]
    for _ln in LANES:
        SEM[_ln] = outer.enter_context(nc.semaphore("lane" + _ln))
    CNT = {k: Ctr() for k in SEM}
    LANE_PREV = {ln: 0 for ln in LANES}

    x2big = sbuf_in(outer, "x2big", [128, NT * D], F32)
    ident = sbuf_in(outer, "ident", [128, 128], F32)
    iot = sbuf_in(outer, "iot", [128, 128], F32)
    emat = sbuf_in(outer, "emat", [128, 8 * 16], F32)
    g01 = sbuf_in(outer, "g01", [128, 16], F32)
    bqk_sb = sbuf_in(outer, "bqk_sb", [128, 8], F32)
    row = sbuf_in(outer, "row", [1, 32], F32)
    nrm = sbuf_in(outer, "nrm", [16, 4], F32)
    negcp = sbuf_in(outer, "negcp", [128, 8], F32)
    scal_bc = sbuf_in(outer, "scal_bc", [128, 16], F32)
    cmv = sbuf_in(outer, "cmv", [128, 1], F32)
    ncmv = sbuf_in(outer, "ncmv", [128, 1], F32)
    epsv = sbuf_in(outer, "epsv", [128, 1], F32)
    stat = sbuf_in(outer, "stat", [128, 12], F32)
    mu = sbuf_in(outer, "mu", [128, 8], F32)
    xmax = sbuf_in(outer, "xmax", [128, NT], F32)
    wmax = sbuf_in(outer, "wmax", [128, 2 * NOC], F32)
    ppmax = sbuf_in(outer, "ppmax", [128, NOC], F32)
    zzmax = sbuf_in(outer, "zzmax", [128, NOC], F32)
    red1 = sbuf_in(outer, "red1", [128, 1], F32)
    ar_sb = sbuf_in(outer, "ar_sb", [1, 8], F32)

    p_a = outer.enter_context(nc.psum_tensor("p_a", [128, 2048], F32))
    p_b = outer.enter_context(nc.psum_tensor("p_b", [128, 2048], F32))

    prog = []

    def em(eng, fn):
        prog.append((eng, fn))

    def dma(eng_name, out, in_, lane=None, new_group=True, slow=False,
            split=1):
        if lane is None:
            lane = "A" if eng_name == "sync" else "G"
        if new_group:
            if LANE_PREV[lane] > 0:
                def wfn(eng, lane=lane, t=LANE_PREV[lane]):
                    eng.wait_ge(SEM[lane], t)
                em(eng_name, wfn)
            LANE_PREV[lane] = CNT[lane].v
        tgt = CNT[lane].add(16 * split)
        LANE_PREV[lane] = tgt

        if split == 1:
            def fn(eng, out=out, in_=in_, lane=lane, slow=slow):
                eng.dma_start(out=out, in_=in_,
                              allow_slow_non_contiguous=slow).then_inc(
                                  SEM[lane], 16)
            em(eng_name, fn)
        else:
            npart = out.shape[0]
            assert npart % split == 0 and in_.shape[0] == npart
            c = npart // split
            for i in range(split):
                def fn(eng, out=out[i * c:(i + 1) * c],
                       in_=in_[i * c:(i + 1) * c], lane=lane, slow=slow):
                    eng.dma_start(out=out, in_=in_,
                                  allow_slow_non_contiguous=slow).then_inc(
                                      SEM[lane], 16)
                em(eng_name, fn)
        return (lane, tgt)

    def wait(eng_name, sem_name, tgt):
        if sem_name == 0 or sem_name is None:
            return
        if isinstance(sem_name, tuple):
            sem_name, tgt = sem_name
        assert tgt <= CNT[sem_name].v, (eng_name, sem_name, tgt, CNT[sem_name].v)
        if tgt <= 0:
            return

        def fn(eng, sem_name=sem_name, tgt=tgt):
            eng.wait_ge(SEM[sem_name], tgt)
        em(eng_name, fn)

    def wait_dma(eng_name, token):
        wait(eng_name, token, None)

    def fence(writer_engines):
        for w in writer_engines:
            for s in ("te", "ve", "se"):
                wait(w, s, CNT[s].v)
            for s in LANES:
                wait(w, s, CNT[s].v)

    def v_em(fn, dep=True):
        if dep and CNT["ve"].v > 0:
            wait("vector", "ve", CNT["ve"].v)
        tgt = CNT["ve"].add(1)

        def wrapped(eng, fn=fn):
            fn(eng).then_inc(SEM["ve"], 1)
        em("vector", wrapped)
        return tgt

    def s_em(fn, dep=True):
        if dep and CNT["se"].v > 0:
            wait("scalar", "se", CNT["se"].v)
        tgt = CNT["se"].add(1)

        def wrapped(eng, fn=fn):
            fn(eng).then_inc(SEM["se"], 1)
        em("scalar", wrapped)
        return tgt

    def t_em(fn):
        tgt = CNT["te"].add(1)

        def wrapped(eng, fn=fn):
            fn(eng).then_inc(SEM["te"], 1)
        em("tensor", wrapped)
        return tgt

    def g_op(fn):
        tgt = CNT["gc"].add(1)

        def wrapped(eng, fn=fn):
            fn(eng).then_inc(SEM["gc"], 1)
        em("gpsimd", wrapped)
        return tgt

    def collective(kind, op, din, dout):
        tgt = CNT["cc"].add(1)

        def fn(g, din=din, dout=dout, kind=kind, op=op):
            g.collective_compute(
                kind, op, replica_groups=[list(range(NC))],
                ins=[din.ap().opt()], outs=[dout.ap().opt()],
            ).then_inc(SEM["cc"], 1)
        em("gpsimd", fn)
        return tgt

    def ln_stats():
        means = stat[:].rearrange("p (g c) -> p g c", c=3)[:, :, 1]
        cvs = stat[:].rearrange("p (g c) -> p g c", c=3)[:, :, 2]
        v_em(lambda v: v.tensor_reduce(out=mu[:, 0:1], in_=means, axis=AX,
                                       op=OP.add))
        v_em(lambda v: v.tensor_scalar(out=mu[:, 0:1], in0=mu[:, 0:1],
                                       scalar1=0.25, scalar2=None,
                                       op0=OP.mult))
        v_em(lambda v: v.tensor_scalar(out=mu[:, 4:8], in0=means,
                                       scalar1=mu[:, 0:1], scalar2=None,
                                       op0=OP.subtract))
        v_em(lambda v: v.tensor_tensor(out=mu[:, 4:8], in0=mu[:, 4:8],
                                       in1=mu[:, 4:8], op=OP.mult))
        v_em(lambda v: v.tensor_reduce(out=mu[:, 1:2], in_=mu[:, 4:8],
                                       axis=AX, op=OP.add))
        v_em(lambda v: v.tensor_reduce(out=mu[:, 4:5], in_=cvs, axis=AX,
                                       op=OP.add))
        v_em(lambda v: v.scalar_tensor_tensor(
            out=mu[:, 4:5], in0=mu[:, 1:2], scalar=256.0, in1=mu[:, 4:5],
            op0=OP.mult, op1=OP.add))
        v_em(lambda v: v.tensor_scalar(out=mu[:, 1:2], in0=mu[:, 4:5],
                                       scalar1=1.0 / 1024.0, scalar2=EPS,
                                       op0=OP.mult, op1=OP.add))
        tv = v_em(lambda v: v.reciprocal(out=mu[:, 1:2], in_=mu[:, 1:2]))
        wait("scalar", "ve", tv)
        ts_ = s_em(lambda s: s.activation(out=mu[:, 1:2], in_=mu[:, 1:2],
                                          func=ACT.Sqrt, bias=0.0,
                                          scale=1.0))
        wait("vector", "se", ts_)

    def allreduce_row(a, n):
        tv = CNT["ve"].v
        wait("gpsimd", "ve", tv)
        t1 = dma("gpsimd", d_arin.ap()[0:1, 0:n], row[0:1, a:a + n])
        wait("gpsimd", t1, None)
        tgt = collective("AllReduce", OP.max, d_arin, d_arout)
        wait("gpsimd", "cc", tgt)
        t2 = dma("gpsimd", ar_sb[0:1, 0:8], d_arout.ap()[0:1, 0:8])
        wait("vector", t2, None)

    def cross_part_max(src_col, out_idx):
        tv = CNT["ve"].v
        wait("tensor", "ve", tv)
        ttr = t_em(lambda t, src_col=src_col: t.transpose(
            p_b[0:1, 0:128], src_col, ident[:]))
        wait("vector", "te", ttr)
        v_em(lambda v, out_idx=out_idx: v.tensor_reduce(
            out=row[0:1, out_idx:out_idx + 1], in_=p_b[0:1, 0:128],
            axis=AX, op=OP.max))

    def bcast_scal(src_a, n, dst_col):
        tv = CNT["ve"].v
        wait("gpsimd", "ve", tv)
        t1 = dma("gpsimd", d_scal[src_a:src_a + n], row[0:1, src_a:src_a + n])
        wait("gpsimd", t1, None)
        return dma("gpsimd", scal_bc[:, dst_col:dst_col + n],
                   d_scal[src_a:src_a + n].partition_broadcast(128))

    _open_stacks = []

    def warmup(n, stat, mov, pdst):
        for i in range(n):
            t_em(lambda t, stat=stat, mov=mov, pdst=pdst: t.matmul(
                pdst, stat, mov, start=True, stop=True))

    def _finish():
        for _st in reversed(_open_stacks):
            _st.close()
        for _ln in LANES:
            wait("sync", _ln, CNT[_ln].v)

        def make_runner(name):
            def _run(eng):
                for e, fn in prog:
                    if e == name:
                        fn(eng)
            return _run

        with outer:
            block = outer.enter_context(nc.Block())
            block.sync(make_runner("sync"))
            block.vector(make_runner("vector"))
            block.scalar(make_runner("scalar"))
            block.tensor(make_runner("tensor"))
            block.gpsimd(make_runner("gpsimd"))
        return nc

    xfT_f32 = x2big
    t_xf = 0
    for fc in range(NFC):
        t_xf = dma("sync", xfT_f32[:, fc * T:(fc + 1) * T],
                   d_x_fT[fc * 128:(fc + 1) * 128, :], new_group=(fc == 0))
    t_bqk = dma("sync", bqk_sb[:], d_bqk.rearrange("(c p) -> p c", p=128), slow=True)

    v_em(lambda v: v.memset(cmv[:], CM))
    v_em(lambda v: v.memset(ncmv[:], -CM))
    v_em(lambda v: v.memset(epsv[:], EPS))
    t_zero = v_em(lambda v: v.memset(ar_sb[:], 0.0))
    v_em(lambda v: v.memset(row[0:1, 0:32], 0.0))
    wait("gpsimd", "ve", t_zero)
    dma("gpsimd", d_arin.ap(), ar_sb[:])

    t_iota = g_op(lambda g: g.iota(out=iot[:], pattern=[[1, 128]],
                                   channel_multiplier=-1,
                                   allow_small_or_imprecise_dtypes=True))
    t_iota2 = g_op(lambda g: g.iota(out=g01[:], pattern=[[0, 16]],
                                    channel_multiplier=1,
                                    allow_small_or_imprecise_dtypes=True))
    wait("vector", "gc", t_iota2)
    v_em(lambda v: v.tensor_scalar(out=ident[:], in0=iot[:], scalar1=0.0,
                                   scalar2=None, op0=OP.is_equal))
    v_em(lambda v: v.tensor_scalar(out=g01[:], in0=g01[:], scalar1=63.5,
                                   scalar2=None, op0=OP.is_gt))
    t_iota3 = g_op(lambda g: g.iota(out=xmax[:, 0:16], pattern=[[1, 16]],
                                    channel_multiplier=0,
                                    allow_small_or_imprecise_dtypes=True))
    wait("vector", "gc", t_iota3)
    v_em(lambda v: v.tensor_tensor(out=wmax[:, 0:16], in0=xmax[:, 0:16],
                                   in1=g01[:], op=OP.subtract))
    last_emat = 0
    for kc in range(8):
        base = float((kc % 2) * 8 + (kc // 2) * 2)
        last_emat = v_em(lambda v, kc=kc, base=base: v.tensor_scalar(
            out=emat[:, kc * 16:(kc + 1) * 16], in0=wmax[:, 0:16],
            scalar1=base, scalar2=None, op0=OP.is_equal))

    s_dq = contextlib.ExitStack()
    _open_stacks.append(s_dq)
    wsl = sbuf_in(s_dq, "wsl", [128, FF], F32)
    dq_s = [sbuf_in(s_dq, "dq_s%d" % i, [128, 32], F32) for i in range(2)]
    dq_r = [sbuf_in(s_dq, "dq_r%d" % i, [128, 32], F32) for i in range(2)]
    dq_a = [sbuf_in(s_dq, "dq_a%d" % i, [128, D], F32) for i in range(2)]
    dq_w = [sbuf_in(s_dq, "dq_w%d" % i, [128, D], F32) for i in range(2)]
    dq_bf = [sbuf_in(s_dq, "dq_bf%d" % i, [128, D], BF16) for i in range(2)]

    def dequant_weight(d_src, d_ag_in, rows, cols):
        nch = rows // 128
        nblk = cols // 32
        wait("sync", "ve", CNT["ve"].v)
        wait("sync", "se", CNT["se"].v)
        t_ws = {}
        for ch in range(nch):
            t_ws[ch] = dma("sync", wsl[:, ch * cols:(ch + 1) * cols],
                           d_src[ch * 128:(ch + 1) * 128, :],
                           lane="W%d" % (ch % 2), new_group=(ch < 2))
        t_tr = {}
        last_d = {0: 0, 1: 0}
        last_se = 0
        for ch in range(nch):
            b = ch % 2
            qa, qw, qs, qr, qb = dq_a[b], dq_w[b], dq_s[b], dq_r[b], dq_bf[b]
            if ch >= 2:
                wait("vector", "te", t_tr[ch - 2])
            wait("vector", t_ws[ch], None)
            wv_ = wsl[:, ch * cols:(ch + 1) * cols]
            wait("scalar", t_ws[ch], None)
            if ch >= 2:
                wait("scalar", "te", t_tr[ch - 2])
            sab = s_em(lambda s, wv_=wv_, cols=cols, qa=qa: s.activation(
                out=qa[:, 0:cols], in_=wv_, func=ACT.Abs, bias=0.0,
                scale=1.0))
            wait("vector", "se", sab)
            n = cols
            srcbuf = qa
            dstbuf = qw
            while n > nblk * 2:
                v_em(lambda v, n=n, srcbuf=srcbuf, dstbuf=dstbuf:
                     v.tensor_tensor(
                         out=dstbuf[:, 0:n // 2],
                         in0=srcbuf[:, 0:n].rearrange(
                             "p (b two) -> p b two", two=2)[:, :, 0],
                         in1=srcbuf[:, 0:n].rearrange(
                             "p (b two) -> p b two", two=2)[:, :, 1],
                         op=OP.max))
                srcbuf, dstbuf = dstbuf, srcbuf
                n //= 2
            v_em(lambda v, n=n, srcbuf=srcbuf, qa=qa: v.tensor_tensor(
                out=qa[:, 0:nblk],
                in0=srcbuf[:, 0:n].rearrange(
                    "p (b two) -> p b two", two=2)[:, :, 0],
                in1=srcbuf[:, 0:n].rearrange(
                    "p (b two) -> p b two", two=2)[:, :, 1],
                op=OP.max))
            v_em(lambda v, nblk=nblk, qa=qa, qs=qs: v.tensor_scalar(
                out=qs[:, 0:nblk], in0=qa[:, 0:nblk],
                scalar1=1.0 / 7.0, scalar2=1e-12, op0=OP.mult, op1=OP.max))
            v_em(lambda v, nblk=nblk, qs=qs, qr=qr: v.reciprocal(
                out=qr[:, 0:nblk], in_=qs[:, 0:nblk]))
            v_em(lambda v, wv_=wv_, cols=cols, nblk=nblk, qa=qa, qr=qr:
                 v.tensor_tensor(
                     out=qa[:, 0:cols].rearrange("p (b w) -> p b w", b=nblk),
                     in0=wv_.rearrange("p (b w) -> p b w", b=nblk),
                     in1=qr[:, 0:nblk].broadcast_to([128, nblk, 32]),
                     op=OP.mult))
            v_em(lambda v, cols=cols, qa=qa: v.tensor_scalar(
                out=qa[:, 0:cols], in0=qa[:, 0:cols],
                scalar1=CM, scalar2=CM, op0=OP.add, op1=OP.subtract))
            lastv = v_em(lambda v, cols=cols, nblk=nblk, qa=qa, qw=qw,
                         qs=qs: v.tensor_tensor(
                             out=qw[:, 0:cols].rearrange(
                                 "p (b w) -> p b w", b=nblk),
                             in0=qa[:, 0:cols].rearrange(
                                 "p (b w) -> p b w", b=nblk),
                             in1=qs[:, 0:nblk].broadcast_to([128, nblk, 32]),
                             op=OP.mult))
            wait("tensor", "ve", lastv)
            wait("scalar", last_d[b], None)
            for cc_ in range(cols // 128):
                tt_ = t_em(lambda t, cc_=cc_, qw=qw: t.transpose(
                    p_a[:, 0:128], qw[:, cc_ * 128:(cc_ + 1) * 128],
                    ident[:]))
                wait("scalar", "te", tt_)
                last_se = s_em(lambda s, cc_=cc_, qb=qb: s.activation(
                    out=qb[:, cc_ * 128:(cc_ + 1) * 128],
                    in_=p_a[:, 0:128], func=ACT.Identity, bias=0.0,
                    scale=1.0))
                if cc_ + 1 < cols // 128:
                    wait("tensor", "se", last_se)
            t_tr[ch] = CNT["te"].v
            wait("sync", "se", last_se)
            for cc_ in range(cols // 128):
                if nch > 1:
                    dst = d_ag_in[cc_ * 128:(cc_ + 1) * 128,
                                  ch * 128:(ch + 1) * 128]
                else:
                    dst = d_ag_in[cc_ * 128:(cc_ + 1) * 128, 0:128]
                last_d[b] = dma("sync", dst,
                                qb[:, cc_ * 128:(cc_ + 1) * 128],
                                lane="P%d" % b, new_group=(cc_ == 0))
        return (last_d[0], last_d[1])

    t_dq_u = dequant_weight(d_wup, d_agu_in.ap(), FF // NC, D)
    t_dq_g = dequant_weight(d_wgt, d_agg_in.ap(), FF // NC, D)
    t_dq_d = dequant_weight(d_wdn, d_agd_in.ap(), D, FF // NC)

    for tk in t_dq_u + t_dq_g + t_dq_d:
        wait("gpsimd", tk, None)
    t_ag_u = collective("AllGather", OP.bypass, d_agu_in, d_agu_out)
    t_ag_g = collective("AllGather", OP.bypass, d_agg_in, d_agg_out)
    t_ag_d = collective("AllGather", OP.bypass, d_agd_in, d_agd_out)
    if STOP_AFTER <= 0:
        return _finish()

    s_dq.close()

    s_att = contextlib.ExitStack()
    _open_stacks.append(s_att)
    o_n = sbuf_in(s_att, "o_n", [128, NT * 512], F32)
    qkT = sbuf_in(s_att, "qkT", [128, H * T], BF16)
    v_aug = sbuf_in(s_att, "v_aug", [128, NT * HWID], BF16)

    s_p1 = contextlib.ExitStack()
    _open_stacks.append(s_p1)
    wqk_bf = sbuf_in(s_p1, "wqk_bf", [128, NFC * D], BF16)
    wv_bf = sbuf_in(s_p1, "wv_bf", [128, NFC * CD], BF16)
    xfT_bf = sbuf_in(s_p1, "xfT_bf", [128, NFC * T], BF16)
    sqs = x2big
    bvf = x2big[:, 2048:2560]
    wstage = x2big[:, 2560:3584]

    fence(["vector"])
    wait("vector", t_xf, None)
    t_xbf = 0
    for fc in range(NFC):
        t_xbf = v_em(lambda v, fc=fc: v.tensor_copy(
            out=xfT_bf[:, fc * T:(fc + 1) * T],
            in_=xfT_f32[:, fc * T:(fc + 1) * T]))
    wait("sync", "ve", t_xbf)
    t_bvf = dma("sync", bvf, d_bv.partition_broadcast(128))

    def load_cast(dsrc, dst_bf, ncols, nch, stage, prev=0):
        last = prev
        for chn in range(nch):
            if last:
                wait("sync", "ve", last)
            tw = dma("sync", stage[:, 0:ncols],
                     dsrc[chn * 128:(chn + 1) * 128, :])
            wait("vector", tw, None)
            last = v_em(lambda v, chn=chn, ncols=ncols, dst_bf=dst_bf,
                        stage=stage: v.tensor_copy(
                            out=dst_bf[:, chn * ncols:(chn + 1) * ncols],
                            in_=stage[:, 0:ncols]))
        return last

    wait("sync", "ve", t_xbf)

    t_wqk = load_cast(d_wqk, wqk_bf, D, NFC, wstage)
    t_wv = load_cast(d_wv, wv_bf, CD, NFC, wstage, prev=t_wqk)

    if DEBUG:
        wait("gpsimd", "ve", t_wqk)
        dma("gpsimd", dbg["wqkbf"], wqk_bf[:])
        dma("gpsimd", dbg["xfbf"], xfT_bf[:])
        wait("tensor", "G", CNT["G"].v)
    wait("tensor", "ve", t_wqk)
    wait("scalar", t_bqk, None)
    se_qk = {}
    for kc in range(NFC):
        pb = p_a if kc % 2 == 0 else p_b
        if kc >= 2:
            wait("tensor", "se", se_qk[kc - 2])
        for fc in range(NFC):
            for tw in range(4):
                def fn(t, kc=kc, tw=tw, fc=fc, pb=pb):
                    return t.matmul(
                        pb[:, tw * 512:(tw + 1) * 512],
                        wqk_bf[:, fc * D + kc * 128:fc * D + (kc + 1) * 128],
                        xfT_bf[:, fc * T + tw * 512:fc * T + (tw + 1) * 512],
                        start=(fc == 0), stop=(fc == NFC - 1))
                t_em(fn)
        tmm = CNT["te"].v
        wait("scalar", "te", tmm)
        se_qk[kc] = s_em(lambda s, kc=kc, pb=pb: s.activation(
            out=qkT[:, kc * T:(kc + 1) * T], in_=pb[:, 0:T], func=ACT.Identity,
            bias=bqk_sb[:, kc:kc + 1], scale=1.0))

    t_vone = v_em(lambda v: v.memset(v_aug[:], 1.0))
    wait("tensor", "ve", max(t_wv, t_vone))
    wait("tensor", "se", se_qk[NFC - 1])
    wait("vector", t_bvf, None)
    ve_v = {}
    for tt in range(NT):
        pb = p_a if tt % 2 == 0 else p_b
        if tt >= 2:
            wait("tensor", "ve", ve_v[tt - 2])
        for fc in range(NFC):
            def fn(t, tt=tt, fc=fc, pb=pb):
                return t.matmul(
                    pb[:, 0:512],
                    xfT_bf[:, fc * T + tt * 128:fc * T + (tt + 1) * 128],
                    wv_bf[:, fc * CD:(fc + 1) * CD],
                    start=(fc == 0), stop=(fc == NFC - 1))
            t_em(fn)
        tmm = CNT["te"].v
        wait("vector", "te", tmm)
        ve_v[tt] = v_em(lambda v, tt=tt, pb=pb: v.tensor_tensor(
            out=v_aug[:, tt * HWID:(tt + 1) * HWID].rearrange(
                "p (h w) -> p h w", h=H)[:, :, 0:DH],
            in0=pb[:, 0:512].rearrange("p (h w) -> p h w", h=H),
            in1=bvf.rearrange("p (h w) -> p h w", h=H),
            op=OP.add))

    wait("tensor", "ve", ve_v[NT - 1])
    for kc in range(NFC):
        if kc > 0:
            wait("vector", "te", CNT["te"].v)
        tv = v_em(lambda v, kc=kc: v.tensor_tensor(
            out=sqs[:, 0:T], in0=qkT[:, kc * T:(kc + 1) * T],
            in1=qkT[:, kc * T:(kc + 1) * T], op=OP.mult))
        wait("tensor", "ve", max(tv, last_emat))
        for iw in range(4):
            def fn(t, kc=kc, iw=iw):
                return t.matmul(p_a[0:16, iw * 512:(iw + 1) * 512],
                                emat[:, kc * 16:(kc + 1) * 16],
                                sqs[:, iw * 512:(iw + 1) * 512],
                                start=(kc == 0), stop=(kc == NFC - 1))
            t_em(fn)
    t_nmm = CNT["te"].v
    wait("vector", "te", t_nmm)
    t_nred = v_em(lambda v: v.tensor_reduce(out=nrm[:, 0:1], in_=p_a[0:16, :],
                                            axis=AX, op=OP.max))
    wait("tensor", "ve", t_nred)
    t_ntr = t_em(lambda t: t.transpose(p_b[0:1, 0:16], nrm[0:16, 0:1],
                                       ident[0:16, 0:16]))
    wait("vector", "te", t_ntr)
    t_rowcopy = v_em(lambda v: v.tensor_copy(out=row[0:1, 0:16],
                                             in_=p_b[0:1, 0:16]))
    v_em(lambda v: v.tensor_tensor(out=row[0:1, 16:24], in0=row[0:1, 0:8],
                                   in1=row[0:1, 8:16], op=OP.mult))
    tv = v_em(lambda v: v.tensor_scalar(out=row[0:1, 16:24],
                                        in0=row[0:1, 16:24],
                                        scalar1=0.015625, scalar2=None,
                                        op0=OP.mult))
    wait("scalar", "ve", tv)
    t_sq = s_em(lambda s: s.activation(out=row[0:1, 16:24],
                                       in_=row[0:1, 16:24],
                                       func=ACT.Sqrt, bias=0.0, scale=1.0))
    wait("vector", "se", t_sq)
    t_negc = v_em(lambda v: v.tensor_scalar(out=row[0:1, 24:32],
                                            in0=row[0:1, 16:24],
                                            scalar1=-1.0, scalar2=-1.0,
                                            op0=OP.mult, op1=OP.add))
    wait("gpsimd", "ve", t_negc)
    t_ng = dma("gpsimd", d_scal[16:24], row[0:1, 24:32])
    wait("gpsimd", t_ng, None)
    t_ng2 = dma("gpsimd", negcp[:], d_scal[16:24].partition_broadcast(128))
    if STOP_AFTER <= 1:
        return _finish()

    if DEBUG:
        dma("gpsimd", dbg["qkT"], qkT[:])
        dma("gpsimd", dbg["vaug"], v_aug[:])
        dma("gpsimd", dbg["row"], row[:])

    s_p1.close()

    s_p2 = contextlib.ExitStack()
    _open_stacks.append(s_p2)
    ptil = [sbuf_in(s_p2, f"ptil{i}", [128, 1024], BF16) for i in range(4)]
    pvscr = sbuf_in(s_p2, "pvscr", [65, 2 * 2048], F32)

    wait("scalar", t_ng2, None)
    fence(["scalar", "vector"])
    wait("tensor", "ve", t_rowcopy)
    warmup(24, qkT[0:64, 0:128], qkT[0:64, 0:512], p_a[:, 0:512])
    se_exp = None
    exp_hist = {}
    pv_hist = {}
    for hp in range(4):
        qc = 2 * hp
        kcc = 2 * hp + 1
        for ih in range(2):
            it = hp * 2 + ih
            units = [(j, head) for j in range(NT) for head in range(2)]

            def emit_qk(u):
                j, head = units[u]
                uu = (it * NT + j) * 2 + head
                base = head * 64
                pq = (uu % 2) * 1024
                if uu >= 2:
                    wait("tensor", "se", exp_hist[uu - 2])
                for iw in range(2):
                    def fn(t, j=j, iw=iw, base=base, pq=pq, qc=qc, kcc=kcc,
                           ih=ih):
                        return t.matmul(
                            p_a[:, pq + iw * 512:pq + (iw + 1) * 512],
                            qkT[base:base + 64,
                                kcc * T + j * 128:kcc * T + (j + 1) * 128],
                            qkT[base:base + 64,
                                qc * T + ih * 1024 + iw * 512:
                                qc * T + ih * 1024 + (iw + 1) * 512],
                            start=True, stop=True, tile_position=(base, 0))
                    t_em(fn)
                t_qk = CNT["te"].v
                wait("scalar", "te", t_qk)
                if uu >= 4:
                    wait("scalar", "te", pv_hist[uu - 4])
                hg = 2 * hp + head
                exp_hist[uu] = s_em(lambda s, hg=hg, uu=uu, pq=pq:
                                    s.activation(
                    out=ptil[uu % 4][:], in_=p_a[:, pq:pq + 1024],
                    func=ACT.Exp, bias=negcp[:, hg:hg + 1], scale=0.125),
                    dep=False)

            def emit_pv(u):
                j, head = units[u]
                uu = (it * NT + j) * 2 + head
                hg = 2 * hp + head
                wait("tensor", "se", exp_hist[uu])
                for iw in range(2):
                    def fn(t, j=j, head=head, iw=iw, hg=hg, uu=uu):
                        return t.matmul(
                            p_b[0:65, (head * 2 + iw) * 512:
                                (head * 2 + iw + 1) * 512],
                            v_aug[:, j * HWID + hg * 65:
                                  j * HWID + (hg + 1) * 65],
                            ptil[uu % 4][:, iw * 512:(iw + 1) * 512],
                            start=(j == 0), stop=(j == NT - 1))
                    pv_hist[uu] = t_em(fn)

            if PIPE_ATT:
                emit_qk(0)
                emit_qk(1)
                for u in range(2, 32):
                    emit_qk(u)
                    emit_pv(u - 2)
                emit_pv(30)
                emit_pv(31)
            else:
                for u in range(32):
                    emit_qk(u)
                    emit_pv(u)
            se_exp = exp_hist[(it * NT + NT - 1) * 2 + 1]
            t_pvmm = CNT["te"].v
            wait("vector", "te", t_pvmm)
            ve_pvev = v_em(lambda v, it=it: v.tensor_copy(
                out=pvscr[:, (it % 2) * 2048:(it % 2) * 2048 + 2048],
                in_=p_b[0:65, :]))
            wait("tensor", "ve", ve_pvev)
            for head in range(2):
                hg = 2 * hp + head
                for iw in range(2):
                    for q4 in range(4):
                        def fn(t, it=it, head=head, iw=iw, q4=q4):
                            src = pvscr[:, (it % 2) * 2048
                                        + (head * 2 + iw) * 512 + q4 * 128:
                                        (it % 2) * 2048
                                        + (head * 2 + iw) * 512 + q4 * 128
                                        + 128]
                            return t.transpose(
                                p_b[:, q4 * 512:q4 * 512 + 65], src,
                                ident[0:65, 0:65])
                        t_em(fn)
                    t_tr4 = CNT["te"].v
                    wait("vector", "te", t_tr4)
                    vrec = v_em(lambda v: v.reciprocal(
                        out=mu[:, 0:4],
                        in_=p_b[:, 0:2048].rearrange(
                            "p (f c) -> p f c", c=512)[:, :, 64]))
                    last_on = 0
                    for q4 in range(4):
                        itile = ih * 8 + iw * 4 + q4
                        def fn(v, q4=q4, itile=itile, hg=hg):
                            return v.tensor_scalar(
                                out=o_n[:, itile * 512 + hg * 64:
                                        itile * 512 + hg * 64 + 64],
                                in0=p_b[:, q4 * 512:q4 * 512 + 64],
                                scalar1=mu[:, q4:q4 + 1], scalar2=None,
                                op0=OP.mult)
                        last_on = v_em(fn, dep=(q4 == 0))
                    wait("tensor", "ve", last_on)

    se_exp_last = se_exp
    if STOP_AFTER <= 2:
        return _finish()
    if DEBUG:
        tv0 = CNT["ve"].v
        wait("gpsimd", "ve", tv0)
        dma("gpsimd", dbg["on"], o_n[:])
    s_p2.close()

    s_p3 = contextlib.ExitStack()
    _open_stacks.append(s_p3)
    wout_bf = sbuf_in(s_p3, "wout_bf", [128, 4 * D], BF16)
    o_nT = sbuf_in(s_p3, "o_nT", [128, 4 * T], BF16)
    xtm = [sbuf_in(s_p3, "xtm0", [128, D], F32),
           sbuf_in(s_p3, "xtm1", [128, D], F32)]
    vt1 = sbuf_in(s_p3, "vt1", [128, D], F32)
    statb = sbuf_in(s_p3, "statb", [128, NT * 12], F32)
    mub = sbuf_in(s_p3, "mub", [128, 6 * NT], F32)
    ut1 = sbuf_in(s_p3, "ut1", [128, D], F32)
    boutf = sbuf_in(s_p3, "boutf", [128, D], F32)
    g1f = sbuf_in(s_p3, "g1f", [128, D], F32)
    be1f = sbuf_in(s_p3, "be1f", [128, D], F32)
    wstg2 = sbuf_in(s_p3, "wstg2", [128, D], F32)

    fence(["sync", "vector", "scalar"])
    t_boutf = dma("sync", boutf[:], d_bout.partition_broadcast(128))
    t_g1 = dma("sync", g1f[:], d_g1.partition_broadcast(128))
    t_be1 = dma("sync", be1f[:], d_be1.partition_broadcast(128))
    t_wout = load_cast(d_wout, wout_bf, D, 4, wstg2)
    wait("vector", t_be1, None)
    v_em(lambda v: v.tensor_scalar(out=g1f[:], in0=g1f[:], scalar1=R2,
                                   scalar2=None, op0=OP.mult))
    t_gb = v_em(lambda v: v.tensor_scalar(out=be1f[:], in0=be1f[:],
                                          scalar1=R2, scalar2=None,
                                          op0=OP.mult))

    if STOP_AFTER <= 2.3:
        return _finish()
    wait("tensor", "se", se_exp_last)
    last_tse = 0
    tse_h = {}
    for idx in range(4 * NT):
        m, tt = divmod(idx, NT)
        half = (idx % 2) * 512
        if idx >= 2:
            wait("tensor", "se", tse_h[idx - 2])
        ttr = t_em(lambda t, m=m, tt=tt, half=half: t.transpose(
            p_a[:, half:half + 128],
            o_n[:, tt * 512 + m * 128:tt * 512 + (m + 1) * 128],
            ident[:]))
        wait("scalar", "te", ttr)
        tse_h[idx] = last_tse = s_em(lambda s, m=m, tt=tt, half=half:
                                     s.activation(
            out=o_nT[:, m * T + tt * 128:m * T + (tt + 1) * 128],
            in_=p_a[:, half:half + 128], func=ACT.Identity, bias=0.0,
            scale=1.0), dep=False)

    if STOP_AFTER <= 2.5:
        return _finish()
    t_xtm = {}
    for tt in range(2):
        t_xtm[tt] = dma("sync", xtm[tt % 2][:],
                        d_x_tm[tt * 128:(tt + 1) * 128, :],
                        lane="U%d" % (tt % 2))
    x2 = x2big
    wait("tensor", "ve", t_wout)
    wait("tensor", "se", last_tse)
    wait("vector", t_boutf, None)

    def batch_ln():
        g4 = statb[:].rearrange("p (t g c) -> p t g c", g=4, c=3)
        means = g4[:, :, :, 1]
        cvs = g4[:, :, :, 2]
        dev = mub[:, 2 * NT:6 * NT].rearrange("p (t g) -> p t g", g=4)
        v_em(lambda v: v.tensor_reduce(out=mub[:, 0:NT], in_=means, axis=AX,
                                       op=OP.add))
        v_em(lambda v: v.tensor_scalar(out=mub[:, 0:NT], in0=mub[:, 0:NT],
                                       scalar1=0.25, scalar2=None,
                                       op0=OP.mult))
        v_em(lambda v: v.tensor_tensor(
            out=dev, in0=means,
            in1=mub[:, 0:NT].rearrange("p (t o) -> p t o", o=1
                                       ).broadcast_to([128, NT, 4]),
            op=OP.subtract))
        v_em(lambda v: v.tensor_tensor(out=dev, in0=dev, in1=dev,
                                       op=OP.mult))
        v_em(lambda v: v.tensor_reduce(out=mub[:, NT:2 * NT], in_=dev,
                                       axis=AX, op=OP.add))
        v_em(lambda v: v.tensor_reduce(out=mub[:, 2 * NT:3 * NT], in_=cvs,
                                       axis=AX, op=OP.add))
        v_em(lambda v: v.scalar_tensor_tensor(
            out=mub[:, NT:2 * NT], in0=mub[:, NT:2 * NT], scalar=256.0,
            in1=mub[:, 2 * NT:3 * NT], op0=OP.mult, op1=OP.add))
        v_em(lambda v: v.tensor_scalar(out=mub[:, NT:2 * NT],
                                       in0=mub[:, NT:2 * NT],
                                       scalar1=1.0 / 1024.0, scalar2=EPS,
                                       op0=OP.mult, op1=OP.add))
        tv = v_em(lambda v: v.reciprocal(out=mub[:, NT:2 * NT],
                                         in_=mub[:, NT:2 * NT]))
        wait("scalar", "ve", tv)
        ts_ = s_em(lambda s: s.activation(out=mub[:, NT:2 * NT],
                                          in_=mub[:, NT:2 * NT],
                                          func=ACT.Sqrt, bias=0.0,
                                          scale=1.0))
        wait("vector", "se", ts_)

    ve_evac = {}
    for tt in range(NT):
        pb = p_a if tt % 2 == 0 else p_b
        if tt >= 2:
            wait("tensor", "ve", ve_evac[tt - 2])
        for m in range(4):
            for fw in range(2):
                def fn(t, tt=tt, m=m, fw=fw, pb=pb):
                    return t.matmul(
                        pb[:, fw * 512:(fw + 1) * 512],
                        o_nT[:, m * T + tt * 128:m * T + (tt + 1) * 128],
                        wout_bf[:, m * D + fw * 512:m * D + (fw + 1) * 512],
                        start=(m == 0), stop=(m == 3))
                t_em(fn)
        t_mm = CNT["te"].v
        wait("vector", "te", t_mm)
        v_em(lambda v, pb=pb, tt=tt: v.tensor_tensor(
            out=x2[:, tt * D:(tt + 1) * D], in0=pb[:, 0:D], in1=boutf[:],
            op=OP.add))
        v_em(lambda v, tt=tt: v.bn_stats(
            out=statb[:, tt * 12:tt * 12 + 6],
            in_=x2[:, tt * D:tt * D + 512]))
        ve_evac[tt] = v_em(lambda v, tt=tt: v.bn_stats(
            out=statb[:, tt * 12 + 6:tt * 12 + 12],
            in_=x2[:, tt * D + 512:tt * D + 1024]))
    batch_ln()

    for tt in range(NT):
        wait("vector", t_xtm[tt], None)
        v_em(lambda v, tt=tt: v.tensor_scalar(
            out=vt1[:], in0=x2[:, tt * D:(tt + 1) * D],
            scalar1=mub[:, tt:tt + 1], scalar2=mub[:, NT + tt:NT + tt + 1],
            op0=OP.subtract, op1=OP.mult))
        v_em(lambda v: v.tensor_tensor(out=vt1[:], in0=vt1[:], in1=g1f[:],
                                       op=OP.mult))
        v_em(lambda v: v.tensor_tensor(out=vt1[:], in0=vt1[:], in1=be1f[:],
                                       op=OP.add))
        vu = v_em(lambda v, tt=tt: v.scalar_tensor_tensor(
            out=x2[:, tt * D:(tt + 1) * D], in0=xtm[tt % 2][:],
            scalar=SQ2P1, in1=vt1[:], op0=OP.mult, op1=OP.add))
        if tt + 2 < NT:
            wait("sync", "ve", vu)
            t_xtm[tt + 2] = dma("sync", xtm[tt % 2][:],
                                d_x_tm[(tt + 2) * 128:(tt + 3) * 128, :],
                                lane="U%d" % (tt % 2))
        v_em(lambda v, tt=tt: v.bn_stats(
            out=statb[:, tt * 12:tt * 12 + 6],
            in_=x2[:, tt * D:tt * D + 512]))
        v_em(lambda v, tt=tt: v.bn_stats(
            out=statb[:, tt * 12 + 6:tt * 12 + 12],
            in_=x2[:, tt * D + 512:tt * D + 1024]))
    batch_ln()

    ve_x2 = {}
    for tt in range(NT):
        v_em(lambda v, tt=tt: v.tensor_scalar(
            out=x2[:, tt * D:(tt + 1) * D], in0=x2[:, tt * D:(tt + 1) * D],
            scalar1=mub[:, tt:tt + 1], scalar2=mub[:, NT + tt:NT + tt + 1],
            op0=OP.subtract, op1=OP.mult))
        ve_x2[tt] = v_em(lambda v, tt=tt: v.tensor_reduce(
            out=xmax[:, tt:tt + 1], in_=x2[:, tt * D:(tt + 1) * D],
            axis=AX, op=OP.max, apply_absolute_value=True))

    if DEBUG:
        tv0 = CNT["ve"].v
        wait("gpsimd", "ve", tv0)
        dma("gpsimd", dbg["x2"], x2[:])
    s_p3.close()
    s_att.close()
    if STOP_AFTER <= 3:
        return _finish()

    v_em(lambda v: v.tensor_reduce(out=red1[:], in_=xmax[:], axis=AX,
                                   op=OP.max))
    cross_part_max(red1[:], 0)
    allreduce_row(0, 1)
    v_em(lambda v: v.tensor_scalar(out=row[0:1, 1:2], in0=ar_sb[0:1, 0:1],
                                   scalar1=1.0 / 32767.0, scalar2=1e-12,
                                   op0=OP.mult, op1=OP.max))
    v_em(lambda v: v.reciprocal(out=row[0:1, 2:3], in_=row[0:1, 1:2]))
    t_scb = bcast_scal(1, 2, 1)
    if STOP_AFTER <= 4:
        return _finish()

    s_ffn = contextlib.ExitStack()
    _open_stacks.append(s_ffn)
    ftile = [sbuf_in(s_ffn, "ft0", [128, T], F32),
             sbuf_in(s_ffn, "ft1", [128, T], F32)]
    fA = [sbuf_in(s_ffn, "fA0", [128, T], F32),
          sbuf_in(s_ffn, "fA1", [128, T], F32)]

    s_f13 = contextlib.ExitStack()
    _open_stacks.append(s_f13)
    xqT = sbuf_in(s_f13, "xqT", [128, NFC * T], BF16)
    aq = sbuf_in(s_f13, "aq", [128, 256], F32)
    wst = [sbuf_in(s_f13, "wst0", [128, NFC * 128], BF16),
           sbuf_in(s_f13, "wst1", [128, NFC * 128], BF16)]
    fstage = [sbuf_in(s_f13, "fst0", [128, T], F32),
              sbuf_in(s_f13, "fst1", [128, T], F32)]
    sgt = sbuf_in(s_f13, "sgt", [128, T], F32)
    qs16 = sbuf_in(s_f13, "qs16", [128, T], U16)
    ppt = [sbuf_in(s_f13, "pp0", [128, T], F32),
           sbuf_in(s_f13, "pp1", [128, T], F32)]
    ut16 = sbuf_in(s_f13, "ut16", [128, T], F32)

    def wld_src(k):
        wi, oc = divmod(k, NOC)
        d_ag = d_agu_out if wi == 0 else d_agg_out
        rank, lo = oc // 3, (oc % 3) * 128
        return d_ag.ap()[rank, :, lo:lo + 128].rearrange("(c p) o -> p c o",
                                                         p=128)

    t_wld = {}
    fence(["sync"])
    wait("sync", "cc", t_ag_g)
    for k in range(2):
        t_wld[k] = dma("sync", wst[k % 2][:].rearrange("p (c o) -> p c o",
                                                       c=NFC), wld_src(k),
                       lane="W%d" % (k % 2))

    fence(["vector"])
    wait("tensor", t_scb, None)
    wait("tensor", "ve", ve_x2[NT - 1])
    tvq = {}
    for idx in range(NFC * NT):
        fc, tt = divmod(idx, NT)
        half = (idx % 2) * 512
        ah = (idx % 2) * 128
        if idx >= 2:
            wait("tensor", "ve", tvq[idx - 2])
        ttr = t_em(lambda t, fc=fc, tt=tt, half=half: t.transpose(
            p_a[:, half:half + 128],
            x2[:, tt * D + fc * 128:tt * D + (fc + 1) * 128], ident[:]))
        wait("vector", "te", ttr)
        v_em(lambda v, half=half, ah=ah: v.tensor_scalar(
            out=aq[:, ah:ah + 128], in0=p_a[:, half:half + 128],
            scalar1=scal_bc[:, 2:3], scalar2=CM,
            op0=OP.mult, op1=OP.add), dep=False)
        tvq[idx] = v_em(lambda v, fc=fc, tt=tt, ah=ah: v.tensor_scalar(
            out=xqT[:, fc * T + tt * 128:fc * T + (tt + 1) * 128],
            in0=aq[:, ah:ah + 128], scalar1=CM, scalar2=scal_bc[:, 1:2],
            op0=OP.subtract, op1=OP.mult))
    t_xq = CNT["ve"].v
    if STOP_AFTER <= 5:
        return _finish()
    if DEBUG:
        wait("gpsimd", "ve", t_xq)
        dma("gpsimd", dbg["xq"], xqT[:])

    wait("tensor", "ve", t_xq)
    te_f1 = {}
    se_fev = {}
    t_fsp = {}
    ve_fred = {}

    for k in range(2 * NOC):
        wi, oc = divmod(k, NOC)
        d_raw = d_upraw if wi == 0 else d_gtraw
        pb = p_a if k % 2 == 0 else p_b
        wait("tensor", t_wld[k], None)
        if k >= 2:
            wait("tensor", "se", se_fev[k - 2])
        for hc in range(NFC):
            for tw in range(4):
                def fn(t, k=k, tw=tw, hc=hc, pb=pb):
                    return t.matmul(
                        pb[:, tw * 512:(tw + 1) * 512],
                        wst[k % 2][:, hc * 128:(hc + 1) * 128],
                        xqT[:, hc * T + tw * 512:hc * T + (tw + 1) * 512],
                        start=(hc == 0), stop=(hc == NFC - 1))
                t_em(fn)
        te_f1[k] = CNT["te"].v
        if k + 2 < 2 * NOC:
            wait("sync", "te", te_f1[k])
            t_wld[k + 2] = dma("sync",
                               wst[k % 2][:].rearrange("p (c o) -> p c o",
                                                       c=NFC),
                               wld_src(k + 2), lane="W%d" % (k % 2))
        wait("scalar", "te", te_f1[k])
        if k >= 2:
            wait("scalar", t_fsp[k - 2], None)
            wait("scalar", "ve", ve_fred[k - 2])
        se_fev[k] = s_em(lambda s, k=k, pb=pb: s.activation(
            out=fstage[k % 2][:], in_=pb[:, 0:T], func=ACT.Identity,
            bias=0.0, scale=1.0), dep=False)
        wait("vector", "se", se_fev[k])
        ve_fred[k] = v_em(lambda v, k=k: v.tensor_reduce(
            out=wmax[:, k:k + 1], in_=fstage[k % 2][:], axis=AX,
            op=OP.max, apply_absolute_value=True), dep=False)
        wait("sync", "se", se_fev[k])
        t_fsp[k] = dma("sync", d_raw.ap()[oc, :, :], fstage[k % 2][:],
                       lane="P%d" % (k % 2))

    zb2 = [ftile[0], ftile[1], fstage[0], fstage[1]]
    ZL = ("W0", "W1", "A", "G")
    t_f2in = {}
    for oc in range(2):
        t_f2in[oc] = dma("sync", zb2[oc][:], d_gtraw.ap()[oc, :, :],
                         lane=ZL[oc])
    wait("sync", t_fsp[2 * NOC - 1], None)
    wait("sync", t_fsp[2 * NOC - 2], None)
    wait("sync", "ve", CNT["ve"].v)
    for oc in range(2, 4):
        t_f2in[oc] = dma("sync", zb2[oc][:], d_gtraw.ap()[oc, :, :],
                         lane=ZL[oc])

    v_em(lambda v: v.tensor_reduce(out=red1[:], in_=wmax[:, 0:NOC], axis=AX,
                                   op=OP.max))
    cross_part_max(red1[:], 4)
    v_em(lambda v: v.tensor_reduce(out=red1[:], in_=wmax[:, NOC:2 * NOC],
                                   axis=AX, op=OP.max))
    cross_part_max(red1[:], 5)
    allreduce_row(4, 2)
    v_em(lambda v: v.tensor_scalar(out=row[0:1, 6:7], in0=ar_sb[0:1, 0:1],
                                   scalar1=1.0 / 32767.0, scalar2=1e-12,
                                   op0=OP.mult, op1=OP.max))
    v_em(lambda v: v.tensor_scalar(out=row[0:1, 8:9], in0=ar_sb[0:1, 1:2],
                                   scalar1=1.0 / 32767.0, scalar2=1e-12,
                                   op0=OP.mult, op1=OP.max))
    v_em(lambda v: v.reciprocal(out=row[0:1, 7:8], in_=row[0:1, 6:7]))
    v_em(lambda v: v.reciprocal(out=row[0:1, 9:10], in_=row[0:1, 8:9]))
    v_em(lambda v: v.tensor_scalar(out=row[0:1, 10:11], in0=row[0:1, 8:9],
                                   scalar1=1.0 / 65536.0, scalar2=None,
                                   op0=OP.mult))
    t_scb2 = bcast_scal(6, 5, 6)
    if STOP_AFTER <= 6:
        return _finish()

    wait("sync", t_scb2, None)
    wait("vector", t_scb2, None)
    vb2 = v_em(lambda v: v.tensor_scalar(
        out=scal_bc[:, 3:4], in0=scal_bc[:, 8:9], scalar1=-CM, scalar2=None,
        op0=OP.mult))
    t_ppo = {}
    v_pp = {}
    vqs = {}
    for oc in range(NOC):
        wait("scalar", t_f2in[oc], None)
        if oc == 0:
            wait("scalar", t_scb2, None)
            wait("scalar", "ve", vb2)
        if oc >= 2:
            wait("scalar", "ve", v_pp[oc - 2])
        sA = s_em(lambda s, oc=oc: s.activation(
            out=fA[oc % 2][:], in_=zb2[oc % 4][:], func=ACT.Identity,
            bias=cmv[:], scale=scal_bc[:, 9:10]))
        if oc + 4 < NOC:
            wait("sync", "se", sA)
            t_f2in[oc + 4] = dma("sync", zb2[oc % 4][:],
                                 d_gtraw.ap()[oc + 4, :, :],
                                 lane=ZL[oc % 4])
        if oc >= 1:
            wait("scalar", "ve", vqs[oc - 1])
        ssig = s_em(lambda s, oc=oc: s.activation(
            out=sgt[:], in_=fA[oc % 2][:], func=ACT.Sigmoid,
            bias=scal_bc[:, 3:4], scale=scal_bc[:, 8:9]))
        ssc = s_em(lambda s: s.activation(out=sgt[:], in_=sgt[:],
                                          func=ACT.Identity, bias=cmv[:],
                                          scale=65536.0))
        wait("vector", "se", ssc)
        vqs[oc] = v_em(lambda v: v.tensor_scalar(
            out=qs16[:], in0=sgt[:], scalar1=CM, scalar2=65535.0,
            op0=OP.subtract, op1=OP.min))
        if oc >= 2:
            wait("vector", t_ppo[oc - 2], None)
        v_pp[oc] = v_em(lambda v, oc=oc: v.scalar_tensor_tensor(
            out=ppt[oc % 2][:], in0=fA[oc % 2][:], scalar=CM, in1=qs16[:],
            op0=OP.subtract, op1=OP.mult))
        vr = v_em(lambda v, oc=oc: v.tensor_reduce(
            out=ppmax[:, oc:oc + 1], in_=ppt[oc % 2][:], axis=AX, op=OP.max,
            apply_absolute_value=True))
        wait("sync", "ve", vr)
        t_ppo[oc] = dma("sync", d_pp.ap()[oc, :, :], ppt[oc % 2][:],
                        lane="P%d" % (oc % 2))

    t_f3p = {}
    t_f3u = {}
    wait("sync", "se", CNT["se"].v)
    wait("sync", "ve", CNT["ve"].v)
    for oc in range(4):
        t_f3p[oc] = dma("sync", zb2[oc % 4][:], d_pp.ap()[oc, :, :],
                        lane=ZL[oc % 4])
    for oc in range(2):
        t_f3u[oc] = dma("sync", fA[oc % 2][:], d_upraw.ap()[oc, :, :],
                        lane="U%d" % (oc % 2))

    v_em(lambda v: v.tensor_reduce(out=red1[:], in_=ppmax[:], axis=AX,
                                   op=OP.max))
    cross_part_max(red1[:], 11)
    v_em(lambda v: v.tensor_tensor(out=row[0:1, 11:12], in0=row[0:1, 11:12],
                                   in1=row[0:1, 10:11], op=OP.mult))
    allreduce_row(11, 1)
    v_em(lambda v: v.tensor_scalar(out=row[0:1, 12:13], in0=ar_sb[0:1, 0:1],
                                   scalar1=1.0 / 32767.0, scalar2=1e-12,
                                   op0=OP.mult, op1=OP.max))
    v_em(lambda v: v.reciprocal(out=row[0:1, 13:14], in_=row[0:1, 12:13]))
    v_em(lambda v: v.tensor_tensor(out=row[0:1, 13:14], in0=row[0:1, 13:14],
                                   in1=row[0:1, 10:11], op=OP.mult))
    t_scb3 = bcast_scal(12, 2, 12)
    if STOP_AFTER <= 7:
        return _finish()

    wait("sync", t_scb3, None)
    t_zzo = {}
    v_zz = {}
    for oc in range(NOC):
        wait("scalar", t_f3p[oc], None)
        wait("scalar", t_f3u[oc], None)
        if oc == 0:
            wait("scalar", t_scb3, None)
        if oc >= 2:
            wait("scalar", "ve", v_zz[oc - 2])
        sA = s_em(lambda s, oc=oc: s.activation(
            out=zb2[oc % 4][:], in_=zb2[oc % 4][:], func=ACT.Identity,
            bias=cmv[:], scale=scal_bc[:, 13:14]))
        sB = s_em(lambda s, oc=oc: s.activation(
            out=fA[oc % 2][:], in_=fA[oc % 2][:], func=ACT.Identity,
            bias=cmv[:], scale=scal_bc[:, 7:8]))
        if oc >= 1:
            wait("scalar", "ve", v_zz[oc - 1])
        s_ut = s_em(lambda s, oc=oc: s.activation(
            out=ut16[:], in_=fA[oc % 2][:], func=ACT.Identity,
            bias=ncmv[:], scale=1.0))
        wait("vector", "se", s_ut)
        if oc >= 2:
            wait("vector", t_zzo[oc - 2], None)
        else:
            wait("vector", t_ppo[NOC - 2 + oc], None)
        v_zz[oc] = v_em(lambda v, oc=oc: v.scalar_tensor_tensor(
            out=ppt[oc % 2][:], in0=zb2[oc % 4][:], scalar=CM, in1=ut16[:],
            op0=OP.subtract, op1=OP.mult))
        vr = v_em(lambda v, oc=oc: v.tensor_reduce(
            out=zzmax[:, oc:oc + 1], in_=ppt[oc % 2][:], axis=AX, op=OP.max,
            apply_absolute_value=True))
        if oc + 4 < NOC:
            wait("sync", "ve", vr)
            t_f3p[oc + 4] = dma("sync", zb2[oc % 4][:],
                                d_pp.ap()[oc + 4, :, :],
                                lane=ZL[oc % 4])
        if oc + 2 < NOC:
            wait("sync", "ve", vr)
            t_f3u[oc + 2] = dma("sync", fA[oc % 2][:],
                                d_upraw.ap()[oc + 2, :, :],
                                lane="U%d" % (oc % 2))
        wait("sync", "ve", vr)
        t_zzo[oc] = dma("sync", d_zz.ap()[oc, :, :], ppt[oc % 2][:],
                        lane="P%d" % (oc % 2))

    t_f4in = {}
    wait("sync", "se", CNT["se"].v)
    wait("sync", "ve", CNT["ve"].v)
    for oc in range(2):
        t_f4in[oc] = dma("sync", ftile[oc % 2][:], d_zz.ap()[oc, :, :],
                         lane="W%d" % (oc % 2))

    v_em(lambda v: v.tensor_reduce(out=red1[:], in_=zzmax[:], axis=AX,
                                   op=OP.max))
    cross_part_max(red1[:], 14)
    v_em(lambda v: v.tensor_tensor(out=row[0:1, 14:15], in0=row[0:1, 14:15],
                                   in1=row[0:1, 12:13], op=OP.mult))
    v_em(lambda v: v.tensor_tensor(out=row[0:1, 14:15], in0=row[0:1, 14:15],
                                   in1=row[0:1, 6:7], op=OP.mult))
    allreduce_row(14, 1)
    v_em(lambda v: v.tensor_scalar(out=row[0:1, 14:15], in0=ar_sb[0:1, 0:1],
                                   scalar1=1.0 / 32767.0, scalar2=1e-12,
                                   op0=OP.mult, op1=OP.max))
    v_em(lambda v: v.reciprocal(out=row[0:1, 15:16], in_=row[0:1, 14:15]))
    v_em(lambda v: v.tensor_tensor(out=row[0:1, 15:16], in0=row[0:1, 15:16],
                                   in1=row[0:1, 12:13], op=OP.mult))
    v_em(lambda v: v.tensor_tensor(out=row[0:1, 15:16], in0=row[0:1, 15:16],
                                   in1=row[0:1, 6:7], op=OP.mult))
    t_scb4 = bcast_scal(14, 2, 14)
    if STOP_AFTER <= 8:
        return _finish()

    s_f13.close()

    s_f4 = contextlib.ExitStack()
    _open_stacks.append(s_f4)
    qz = sbuf_in(s_f4, "qz", [128, NOC * T], BF16)
    wdnt = [sbuf_in(s_f4, "wdn%d" % i, [128, D], BF16)
            for i in range(4)]
    ot = sbuf_in(s_f4, "ot", [128, D], F32)

    fence(["vector", "sync", "scalar"])
    wait("scalar", t_scb4, None)
    t_qz = 0
    vz4 = {}
    for oc in range(NOC):
        wait("scalar", t_f4in[oc], None)
        if oc >= 2:
            wait("scalar", "ve", vz4[oc - 2])
        sA = s_em(lambda s, oc=oc: s.activation(
            out=fA[oc % 2][:], in_=ftile[oc % 2][:], func=ACT.Identity,
            bias=cmv[:], scale=scal_bc[:, 15:16]))
        if oc + 2 < NOC:
            wait("sync", "se", sA)
            t_f4in[oc + 2] = dma("sync", ftile[oc % 2][:],
                                 d_zz.ap()[oc + 2, :, :],
                                 lane="W%d" % (oc % 2))
        wait("vector", "se", sA)
        vz4[oc] = t_qz = v_em(lambda v, oc=oc: v.tensor_scalar(
            out=qz[:, oc * T:(oc + 1) * T], in0=fA[oc % 2][:],
            scalar1=CM, scalar2=None, op0=OP.subtract))

    wait("sync", "cc", t_ag_d)

    def load_wdn(oc):
        return dma("sync", wdnt[oc % 4][:],
                   d_agd_out.ap()[oc // 3,
                                  (oc % 3) * 128:(oc % 3) * 128 + 128, :],
                   lane=("U0", "U1", "P0", "P1")[oc % 4])

    def otb(tsub):
        return ftile[tsub // 2][:, (tsub % 2) * D:(tsub % 2 + 1) * D]

    t_yst = {}
    t_wl = {oc: load_wdn(oc) for oc in range(4)}
    se4_prev = 0
    for grp in range(4):
        te_f4 = {}
        if grp == 0:
            warmup(24, ident[:], ident[:, 0:128], p_b[:, 1536:1664])
        for oc in range(NOC):
            wait("tensor", t_wl[oc], None)
            if oc == 0:
                wait("tensor", "ve", se4_prev)
            if grp == 0:
                wait("tensor", "ve", vz4[oc])
            elif oc == 0:
                wait("tensor", "ve", t_qz)
            for tsub in range(4):
                tt = grp * 4 + tsub
                for fw in range(2):
                    def fn(t, oc=oc, tt=tt, tsub=tsub, fw=fw):
                        pb = p_a if tsub < 2 else p_b
                        off = (tsub % 2) * 1024 + fw * 512
                        return t.matmul(
                            pb[:, off:off + 512],
                            qz[:, oc * T + tt * 128:oc * T + (tt + 1) * 128],
                            wdnt[oc % 4][:, fw * 512:(fw + 1) * 512],
                            start=(oc == 0), stop=(oc == NOC - 1))
                    te_f4[oc] = t_em(fn)
            if oc + 4 < NOC:
                wait("sync", "te", te_f4[oc])
                t_wl[oc + 4] = load_wdn(oc + 4)
        t_dmm = te_f4[NOC - 1]
        sev4 = {}
        for tsub in range(4):
            pb = p_a if tsub < 2 else p_b
            off = (tsub % 2) * 1024
            tt = grp * 4 + tsub
            wait("vector", "te", t_dmm)
            if grp > 0:
                wait("vector", t_yst[(grp - 1) * 4 + tsub], None)
            sev4[tsub] = v_em(lambda v, pb=pb, off=off, tsub=tsub, tt=tt:
                              v.scalar_tensor_tensor(
                out=otb(tsub), in0=pb[:, off:off + D],
                scalar=scal_bc[:, 14:15], in1=x2[:, tt * D:(tt + 1) * D],
                op0=OP.mult, op1=OP.add))
        se4_prev = sev4[3]
        if grp < 3:
            wait("sync", "te", t_dmm)
            for oc in range(4):
                t_wl[oc] = load_wdn(oc)
        for tsub in range(4):
            tt = grp * 4 + tsub
            v_em(lambda v, tsub=tsub: v.bn_stats(
                out=stat[:, 0:6], in_=otb(tsub)[:, 0:512]))
            v_em(lambda v, tsub=tsub: v.bn_stats(
                out=stat[:, 6:12], in_=otb(tsub)[:, 512:1024]))
            ln_stats()
            vf = v_em(lambda v, tsub=tsub: v.tensor_scalar(
                out=otb(tsub), in0=otb(tsub), scalar1=mu[:, 0:1],
                scalar2=mu[:, 1:2], op0=OP.subtract, op1=OP.mult))
            wait("sync", "ve", vf)
            t_yst[tt] = dma("sync", d_y[tt * 128:(tt + 1) * 128, :],
                            otb(tsub), lane="A")

    s_f4.close()
    s_ffn.close()

    if DEBUG:
        for _ln in LANES:
            wait("gpsimd", _ln, CNT[_ln].v)
        for nm, dsrc in [("up", d_upraw), ("gt", d_gtraw), ("pp", d_pp),
                         ("zz", d_zz)]:
            for oc in range(NOC):
                dma("gpsimd", dbg[nm][oc, :, :], dsrc.ap()[oc, :, :])
        dma("gpsimd", dbg["scal"], d_scal[0:32])
        wait("sync", "G", CNT["G"].v)

    return _finish()


def kernel(**inputs):
    x = np.asarray(inputs["L0_attn_x"], dtype=np.float32)
    W_qk = np.asarray(inputs["L0_attn_W_qk"], dtype=np.float32)
    b_qk = np.asarray(inputs["L0_attn_b_qk"], dtype=np.float32)
    W_v = np.asarray(inputs["L0_attn_W_v"], dtype=np.float32)
    b_v = np.asarray(inputs["L0_attn_b_v"], dtype=np.float32)
    W_out = np.asarray(inputs["L0_attn_W_out"], dtype=np.float32)
    b_out = np.asarray(inputs["L0_attn_b_out"], dtype=np.float32)
    g1 = np.asarray(inputs["L0_attn_ln_gamma"], dtype=np.float32)
    be1 = np.asarray(inputs["L0_attn_ln_beta"], dtype=np.float32)
    w_up = np.asarray(inputs["L0_ffn_w_up"], dtype=np.float32)
    w_gate = np.asarray(inputs["L0_ffn_w_gate"], dtype=np.float32)
    w_down = np.asarray(inputs["L0_ffn_w_down"], dtype=np.float32)

    perm = []
    for g in range(4):
        for h in (2 * g, 2 * g + 1):
            perm += [h * 128 + 2 * d for d in range(64)]
        for h in (2 * g, 2 * g + 1):
            perm += [h * 128 + 2 * d + 1 for d in range(64)]
    perm = np.asarray(perm)
    Wqk_r = np.ascontiguousarray(W_qk[:, perm])
    bqk_r = np.ascontiguousarray(b_qk[perm])

    nc = build_nc()

    in_maps = []
    for c in range(NC):
        xc = np.ascontiguousarray(x[c])
        in_maps.append({
            "x_tm": xc,
            "x_fT": np.ascontiguousarray(xc.T),
            "wqk_r": Wqk_r,
            "bqk_r": bqk_r,
            "wv": W_v,
            "bv": b_v,
            "wout": W_out,
            "bout": b_out,
            "g1": g1,
            "be1": be1,
            "wup_sl": np.ascontiguousarray(
                w_up[c * (FF // NC):(c + 1) * (FF // NC)]),
            "wgt_sl": np.ascontiguousarray(
                w_gate[c * (FF // NC):(c + 1) * (FF // NC)]),
            "wdn_sl": np.ascontiguousarray(
                w_down[:, c * (FF // NC):(c + 1) * (FF // NC)]),
        })
    res = run_bass_kernel_spmd(nc, in_maps, core_ids=list(range(NC)))
    return np.stack([np.asarray(r["y_out"]) for r in res.results], axis=0)


# version 26 (final)
# speedup vs baseline: 1.0223x; 1.0223x over previous
import contextlib
import numpy as np
import ml_dtypes
import concourse.bass as bass
import concourse.mybir as mybir
from concourse.alu_op_type import AluOpType as OP
from concourse.bass_utils import run_bass_kernel_spmd

F32 = mybir.dt.float32
BF16 = mybir.dt.bfloat16
I16 = mybir.dt.int16
U16 = mybir.dt.uint16
ACT = mybir.ActivationFunctionType
AX = mybir.AxisListType.X
BF = ml_dtypes.bfloat16

T = 2048
D = 1024
H = 8
DH = 64
CD = 512
FF = 3072
NC = 8
CM = 12582912.0
R2 = 0.7071067811865476
SQ2P1 = 1.0 + R2
EPS = 1e-5
NT = T // 128
NFC = D // 128
NOC = FF // 128
HWID = H * 65
DEBUG = False
PIPE_ATT = True
STOP_AFTER = 99


class Ctr:
    def __init__(self):
        self.v = 0

    def add(self, n):
        self.v += n
        return self.v


def build_nc():
    nc = bass.Bass("TRN2", target_bir_lowering=False)

    d_x_tm = nc.dram_tensor("x_tm", [T, D], F32, kind="ExternalInput").ap()
    d_x_fT = nc.dram_tensor("x_fT", [D, T], F32, kind="ExternalInput").ap()
    d_wqk = nc.dram_tensor("wqk_r", [D, D], F32, kind="ExternalInput").ap()
    d_bqk = nc.dram_tensor("bqk_r", [D], F32, kind="ExternalInput").ap()
    d_wv = nc.dram_tensor("wv", [D, CD], F32, kind="ExternalInput").ap()
    d_bv = nc.dram_tensor("bv", [CD], F32, kind="ExternalInput").ap()
    d_wout = nc.dram_tensor("wout", [CD, D], F32, kind="ExternalInput").ap()
    d_bout = nc.dram_tensor("bout", [D], F32, kind="ExternalInput").ap()
    d_g1 = nc.dram_tensor("g1", [D], F32, kind="ExternalInput").ap()
    d_be1 = nc.dram_tensor("be1", [D], F32, kind="ExternalInput").ap()
    d_wup = nc.dram_tensor("wup_sl", [FF // NC, D], F32, kind="ExternalInput").ap()
    d_wgt = nc.dram_tensor("wgt_sl", [FF // NC, D], F32, kind="ExternalInput").ap()
    d_wdn = nc.dram_tensor("wdn_sl", [D, FF // NC], F32, kind="ExternalInput").ap()
    d_y = nc.dram_tensor("y_out", [T, D], F32, kind="ExternalOutput").ap()

    d_agu_in = nc.dram_tensor("agu_in", [D, FF // NC], BF16)
    d_agu_out = nc.dram_tensor("agu_out", [NC, D, FF // NC], BF16)
    d_agg_in = nc.dram_tensor("agg_in", [D, FF // NC], BF16)
    d_agg_out = nc.dram_tensor("agg_out", [NC, D, FF // NC], BF16)
    d_agd_in = nc.dram_tensor("agd_in", [FF // NC, D], BF16)
    d_agd_out = nc.dram_tensor("agd_out", [NC, FF // NC, D], BF16)
    d_upraw = nc.dram_tensor("upraw", [NOC, 128, T], F32)
    d_gtraw = nc.dram_tensor("gtraw", [NOC, 128, T], F32)
    d_pp = nc.dram_tensor("ppd", [NOC, 128, T], F32)
    d_zz = nc.dram_tensor("zzd", [NOC, 128, T], F32)
    if DEBUG:
        dbg = {
            "qkT": nc.dram_tensor("dbg_qkT", [128, H * T], BF16,
                                  kind="ExternalOutput").ap(),
            "vaug": nc.dram_tensor("dbg_vaug", [128, NT * HWID], BF16,
                                   kind="ExternalOutput").ap(),
            "row": nc.dram_tensor("dbg_row", [1, 32], F32,
                                  kind="ExternalOutput").ap(),
            "on": nc.dram_tensor("dbg_on", [128, NT * 512], F32,
                                 kind="ExternalOutput").ap(),
            "x2": nc.dram_tensor("dbg_x2", [128, NT * D], F32,
                                 kind="ExternalOutput").ap(),
            "xq": nc.dram_tensor("dbg_xq", [128, NFC * T], BF16,
                                 kind="ExternalOutput").ap(),
            "up": nc.dram_tensor("dbg_up", [NOC, 128, T], F32,
                                 kind="ExternalOutput").ap(),
            "gt": nc.dram_tensor("dbg_gt", [NOC, 128, T], F32,
                                 kind="ExternalOutput").ap(),
            "pp": nc.dram_tensor("dbg_pp", [NOC, 128, T], F32,
                                 kind="ExternalOutput").ap(),
            "zz": nc.dram_tensor("dbg_zz", [NOC, 128, T], F32,
                                 kind="ExternalOutput").ap(),
            "scal": nc.dram_tensor("dbg_scal", [32], F32,
                                   kind="ExternalOutput").ap(),
            "wqkbf": nc.dram_tensor("dbg_wqkbf", [128, NFC * D], BF16,
                                    kind="ExternalOutput").ap(),
            "xfbf": nc.dram_tensor("dbg_xfbf", [128, NFC * T], BF16,
                                   kind="ExternalOutput").ap(),
        }
    d_arin = nc.dram_tensor("arin", [1, 8], F32)
    d_arout = nc.dram_tensor("arout", [1, 8], F32)
    d_scal = nc.dram_tensor("scald", [32], F32)

    outer = contextlib.ExitStack()

    def sbuf_in(stack, name, shape, dtype=F32):
        return stack.enter_context(nc.sbuf_tensor(name, shape, dtype))

    ld = outer.enter_context(nc.semaphore("ld"))
    gd = outer.enter_context(nc.semaphore("gd"))
    ccs = outer.enter_context(nc.semaphore("ccs"))
    te = outer.enter_context(nc.semaphore("te"))
    ve = outer.enter_context(nc.semaphore("ve"))
    se = outer.enter_context(nc.semaphore("se"))
    gc2 = outer.enter_context(nc.semaphore("gc"))
    SEM = {"ld": ld, "gd": gd, "cc": ccs, "te": te, "ve": ve, "se": se,
           "gc": gc2}
    LANES = ["A", "W0", "W1", "U0", "U1", "P0", "P1", "G"]
    for _ln in LANES:
        SEM[_ln] = outer.enter_context(nc.semaphore("lane" + _ln))
    CNT = {k: Ctr() for k in SEM}
    LANE_PREV = {ln: 0 for ln in LANES}

    x2big = sbuf_in(outer, "x2big", [128, NT * D], F32)
    ident = sbuf_in(outer, "ident", [128, 128], F32)
    iot = sbuf_in(outer, "iot", [128, 128], F32)
    emat = sbuf_in(outer, "emat", [128, 8 * 16], F32)
    g01 = sbuf_in(outer, "g01", [128, 16], F32)
    bqk_sb = sbuf_in(outer, "bqk_sb", [128, 8], F32)
    row = sbuf_in(outer, "row", [1, 32], F32)
    nrm = sbuf_in(outer, "nrm", [16, 4], F32)
    negcp = sbuf_in(outer, "negcp", [128, 8], F32)
    scal_bc = sbuf_in(outer, "scal_bc", [128, 16], F32)
    cmv = sbuf_in(outer, "cmv", [128, 1], F32)
    ncmv = sbuf_in(outer, "ncmv", [128, 1], F32)
    epsv = sbuf_in(outer, "epsv", [128, 1], F32)
    stat = sbuf_in(outer, "stat", [128, 12], F32)
    mu = sbuf_in(outer, "mu", [128, 8], F32)
    xmax = sbuf_in(outer, "xmax", [128, NT], F32)
    wmax = sbuf_in(outer, "wmax", [128, 2 * NOC], F32)
    ppmax = sbuf_in(outer, "ppmax", [128, NOC], F32)
    zzmax = sbuf_in(outer, "zzmax", [128, NOC], F32)
    red1 = sbuf_in(outer, "red1", [128, 1], F32)
    ar_sb = sbuf_in(outer, "ar_sb", [1, 8], F32)

    p_a = outer.enter_context(nc.psum_tensor("p_a", [128, 2048], F32))
    p_b = outer.enter_context(nc.psum_tensor("p_b", [128, 2048], F32))

    prog = []

    def em(eng, fn):
        prog.append((eng, fn))

    def dma(eng_name, out, in_, lane=None, new_group=True, slow=False,
            split=1):
        if lane is None:
            lane = "A" if eng_name == "sync" else "G"
        if new_group:
            if LANE_PREV[lane] > 0:
                def wfn(eng, lane=lane, t=LANE_PREV[lane]):
                    eng.wait_ge(SEM[lane], t)
                em(eng_name, wfn)
            LANE_PREV[lane] = CNT[lane].v
        tgt = CNT[lane].add(16 * split)
        LANE_PREV[lane] = tgt

        if split == 1:
            def fn(eng, out=out, in_=in_, lane=lane, slow=slow):
                eng.dma_start(out=out, in_=in_,
                              allow_slow_non_contiguous=slow).then_inc(
                                  SEM[lane], 16)
            em(eng_name, fn)
        else:
            npart = out.shape[0]
            assert npart % split == 0 and in_.shape[0] == npart
            c = npart // split
            for i in range(split):
                def fn(eng, out=out[i * c:(i + 1) * c],
                       in_=in_[i * c:(i + 1) * c], lane=lane, slow=slow):
                    eng.dma_start(out=out, in_=in_,
                                  allow_slow_non_contiguous=slow).then_inc(
                                      SEM[lane], 16)
                em(eng_name, fn)
        return (lane, tgt)

    def wait(eng_name, sem_name, tgt):
        if sem_name == 0 or sem_name is None:
            return
        if isinstance(sem_name, tuple):
            sem_name, tgt = sem_name
        assert tgt <= CNT[sem_name].v, (eng_name, sem_name, tgt, CNT[sem_name].v)
        if tgt <= 0:
            return

        def fn(eng, sem_name=sem_name, tgt=tgt):
            eng.wait_ge(SEM[sem_name], tgt)
        em(eng_name, fn)

    def wait_dma(eng_name, token):
        wait(eng_name, token, None)

    def fence(writer_engines):
        for w in writer_engines:
            for s in ("te", "ve", "se"):
                wait(w, s, CNT[s].v)
            for s in LANES:
                wait(w, s, CNT[s].v)

    def v_em(fn, dep=True):
        if dep and CNT["ve"].v > 0:
            wait("vector", "ve", CNT["ve"].v)
        tgt = CNT["ve"].add(1)

        def wrapped(eng, fn=fn):
            fn(eng).then_inc(SEM["ve"], 1)
        em("vector", wrapped)
        return tgt

    def s_em(fn, dep=True):
        if dep and CNT["se"].v > 0:
            wait("scalar", "se", CNT["se"].v)
        tgt = CNT["se"].add(1)

        def wrapped(eng, fn=fn):
            fn(eng).then_inc(SEM["se"], 1)
        em("scalar", wrapped)
        return tgt

    def t_em(fn):
        tgt = CNT["te"].add(1)

        def wrapped(eng, fn=fn):
            fn(eng).then_inc(SEM["te"], 1)
        em("tensor", wrapped)
        return tgt

    def g_op(fn):
        tgt = CNT["gc"].add(1)

        def wrapped(eng, fn=fn):
            fn(eng).then_inc(SEM["gc"], 1)
        em("gpsimd", wrapped)
        return tgt

    def collective(kind, op, din, dout):
        tgt = CNT["cc"].add(1)

        def fn(g, din=din, dout=dout, kind=kind, op=op):
            g.collective_compute(
                kind, op, replica_groups=[list(range(NC))],
                ins=[din.ap().opt()], outs=[dout.ap().opt()],
            ).then_inc(SEM["cc"], 1)
        em("gpsimd", fn)
        return tgt

    def ln_stats():
        means = stat[:].rearrange("p (g c) -> p g c", c=3)[:, :, 1]
        cvs = stat[:].rearrange("p (g c) -> p g c", c=3)[:, :, 2]
        v_em(lambda v: v.tensor_reduce(out=mu[:, 0:1], in_=means, axis=AX,
                                       op=OP.add))
        v_em(lambda v: v.tensor_scalar(out=mu[:, 0:1], in0=mu[:, 0:1],
                                       scalar1=0.25, scalar2=None,
                                       op0=OP.mult))
        v_em(lambda v: v.tensor_scalar(out=mu[:, 4:8], in0=means,
                                       scalar1=mu[:, 0:1], scalar2=None,
                                       op0=OP.subtract))
        v_em(lambda v: v.tensor_tensor(out=mu[:, 4:8], in0=mu[:, 4:8],
                                       in1=mu[:, 4:8], op=OP.mult))
        v_em(lambda v: v.tensor_reduce(out=mu[:, 1:2], in_=mu[:, 4:8],
                                       axis=AX, op=OP.add))
        v_em(lambda v: v.tensor_reduce(out=mu[:, 4:5], in_=cvs, axis=AX,
                                       op=OP.add))
        v_em(lambda v: v.scalar_tensor_tensor(
            out=mu[:, 4:5], in0=mu[:, 1:2], scalar=256.0, in1=mu[:, 4:5],
            op0=OP.mult, op1=OP.add))
        v_em(lambda v: v.tensor_scalar(out=mu[:, 1:2], in0=mu[:, 4:5],
                                       scalar1=1.0 / 1024.0, scalar2=EPS,
                                       op0=OP.mult, op1=OP.add))
        tv = v_em(lambda v: v.reciprocal(out=mu[:, 1:2], in_=mu[:, 1:2]))
        wait("scalar", "ve", tv)
        ts_ = s_em(lambda s: s.activation(out=mu[:, 1:2], in_=mu[:, 1:2],
                                          func=ACT.Sqrt, bias=0.0,
                                          scale=1.0))
        wait("vector", "se", ts_)

    def allreduce_row(a, n):
        tv = CNT["ve"].v
        wait("gpsimd", "ve", tv)
        t1 = dma("gpsimd", d_arin.ap()[0:1, 0:n], row[0:1, a:a + n])
        wait("gpsimd", t1, None)
        tgt = collective("AllReduce", OP.max, d_arin, d_arout)
        wait("gpsimd", "cc", tgt)
        t2 = dma("gpsimd", ar_sb[0:1, 0:8], d_arout.ap()[0:1, 0:8])
        wait("vector", t2, None)

    def cross_part_max(src_col, out_idx):
        tv = CNT["ve"].v
        wait("tensor", "ve", tv)
        ttr = t_em(lambda t, src_col=src_col: t.transpose(
            p_b[0:1, 0:128], src_col, ident[:]))
        wait("vector", "te", ttr)
        v_em(lambda v, out_idx=out_idx: v.tensor_reduce(
            out=row[0:1, out_idx:out_idx + 1], in_=p_b[0:1, 0:128],
            axis=AX, op=OP.max))

    def bcast_scal(src_a, n, dst_col):
        tv = CNT["ve"].v
        wait("gpsimd", "ve", tv)
        t1 = dma("gpsimd", d_scal[src_a:src_a + n], row[0:1, src_a:src_a + n])
        wait("gpsimd", t1, None)
        return dma("gpsimd", scal_bc[:, dst_col:dst_col + n],
                   d_scal[src_a:src_a + n].partition_broadcast(128))

    _open_stacks = []

    def warmup(n, stat, mov, pdst):
        for i in range(n):
            t_em(lambda t, stat=stat, mov=mov, pdst=pdst: t.matmul(
                pdst, stat, mov, start=True, stop=True))

    def _finish():
        for _st in reversed(_open_stacks):
            _st.close()
        for _ln in LANES:
            wait("sync", _ln, CNT[_ln].v)

        def make_runner(name):
            def _run(eng):
                for e, fn in prog:
                    if e == name:
                        fn(eng)
            return _run

        with outer:
            block = outer.enter_context(nc.Block())
            block.sync(make_runner("sync"))
            block.vector(make_runner("vector"))
            block.scalar(make_runner("scalar"))
            block.tensor(make_runner("tensor"))
            block.gpsimd(make_runner("gpsimd"))
        return nc

    xfT_f32 = x2big
    t_xf = 0
    for fc in range(NFC):
        t_xf = dma("sync", xfT_f32[:, fc * T:(fc + 1) * T],
                   d_x_fT[fc * 128:(fc + 1) * 128, :], new_group=(fc == 0))
    t_bqk = dma("sync", bqk_sb[:], d_bqk.rearrange("(c p) -> p c", p=128), slow=True)

    v_em(lambda v: v.memset(cmv[:], CM))
    v_em(lambda v: v.memset(ncmv[:], -CM))
    v_em(lambda v: v.memset(epsv[:], EPS))
    t_zero = v_em(lambda v: v.memset(ar_sb[:], 0.0))
    v_em(lambda v: v.memset(row[0:1, 0:32], 0.0))
    wait("gpsimd", "ve", t_zero)
    dma("gpsimd", d_arin.ap(), ar_sb[:])

    t_iota = g_op(lambda g: g.iota(out=iot[:], pattern=[[1, 128]],
                                   channel_multiplier=-1,
                                   allow_small_or_imprecise_dtypes=True))
    t_iota2 = g_op(lambda g: g.iota(out=g01[:], pattern=[[0, 16]],
                                    channel_multiplier=1,
                                    allow_small_or_imprecise_dtypes=True))
    wait("vector", "gc", t_iota2)
    v_em(lambda v: v.tensor_scalar(out=ident[:], in0=iot[:], scalar1=0.0,
                                   scalar2=None, op0=OP.is_equal))
    v_em(lambda v: v.tensor_scalar(out=g01[:], in0=g01[:], scalar1=63.5,
                                   scalar2=None, op0=OP.is_gt))
    t_iota3 = g_op(lambda g: g.iota(out=xmax[:, 0:16], pattern=[[1, 16]],
                                    channel_multiplier=0,
                                    allow_small_or_imprecise_dtypes=True))
    wait("vector", "gc", t_iota3)
    v_em(lambda v: v.tensor_tensor(out=wmax[:, 0:16], in0=xmax[:, 0:16],
                                   in1=g01[:], op=OP.subtract))
    last_emat = 0
    for kc in range(8):
        base = float((kc % 2) * 8 + (kc // 2) * 2)
        last_emat = v_em(lambda v, kc=kc, base=base: v.tensor_scalar(
            out=emat[:, kc * 16:(kc + 1) * 16], in0=wmax[:, 0:16],
            scalar1=base, scalar2=None, op0=OP.is_equal))

    s_dq = contextlib.ExitStack()
    _open_stacks.append(s_dq)
    wsl = sbuf_in(s_dq, "wsl", [128, FF], F32)
    dq_s = [sbuf_in(s_dq, "dq_s%d" % i, [128, 32], F32) for i in range(2)]
    dq_r = [sbuf_in(s_dq, "dq_r%d" % i, [128, 32], F32) for i in range(2)]
    dq_a = [sbuf_in(s_dq, "dq_a%d" % i, [128, D], F32) for i in range(2)]
    dq_w = [sbuf_in(s_dq, "dq_w%d" % i, [128, D], F32) for i in range(2)]
    dq_bf = [sbuf_in(s_dq, "dq_bf%d" % i, [128, D], BF16) for i in range(2)]

    def dequant_weight(d_src, d_ag_in, rows, cols):
        nch = rows // 128
        nblk = cols // 32
        wait("sync", "ve", CNT["ve"].v)
        wait("sync", "se", CNT["se"].v)
        t_ws = {}
        for ch in range(nch):
            t_ws[ch] = dma("sync", wsl[:, ch * cols:(ch + 1) * cols],
                           d_src[ch * 128:(ch + 1) * 128, :],
                           lane="W%d" % (ch % 2), new_group=(ch < 2))
        t_tr = {}
        last_d = {0: 0, 1: 0}
        last_se = 0
        for ch in range(nch):
            b = ch % 2
            qa, qw, qs, qr, qb = dq_a[b], dq_w[b], dq_s[b], dq_r[b], dq_bf[b]
            if ch >= 2:
                wait("vector", "te", t_tr[ch - 2])
            wait("vector", t_ws[ch], None)
            wv_ = wsl[:, ch * cols:(ch + 1) * cols]
            wait("scalar", t_ws[ch], None)
            if ch >= 2:
                wait("scalar", "te", t_tr[ch - 2])
            sab = s_em(lambda s, wv_=wv_, cols=cols, qa=qa: s.activation(
                out=qa[:, 0:cols], in_=wv_, func=ACT.Abs, bias=0.0,
                scale=1.0))
            wait("vector", "se", sab)
            n = cols
            srcbuf = qa
            dstbuf = qw
            while n > nblk * 2:
                v_em(lambda v, n=n, srcbuf=srcbuf, dstbuf=dstbuf:
                     v.tensor_tensor(
                         out=dstbuf[:, 0:n // 2],
                         in0=srcbuf[:, 0:n].rearrange(
                             "p (b two) -> p b two", two=2)[:, :, 0],
                         in1=srcbuf[:, 0:n].rearrange(
                             "p (b two) -> p b two", two=2)[:, :, 1],
                         op=OP.max))
                srcbuf, dstbuf = dstbuf, srcbuf
                n //= 2
            v_em(lambda v, n=n, srcbuf=srcbuf, qa=qa: v.tensor_tensor(
                out=qa[:, 0:nblk],
                in0=srcbuf[:, 0:n].rearrange(
                    "p (b two) -> p b two", two=2)[:, :, 0],
                in1=srcbuf[:, 0:n].rearrange(
                    "p (b two) -> p b two", two=2)[:, :, 1],
                op=OP.max))
            v_em(lambda v, nblk=nblk, qa=qa, qs=qs: v.tensor_scalar(
                out=qs[:, 0:nblk], in0=qa[:, 0:nblk],
                scalar1=1.0 / 7.0, scalar2=1e-12, op0=OP.mult, op1=OP.max))
            v_em(lambda v, nblk=nblk, qs=qs, qr=qr: v.reciprocal(
                out=qr[:, 0:nblk], in_=qs[:, 0:nblk]))
            v_em(lambda v, wv_=wv_, cols=cols, nblk=nblk, qa=qa, qr=qr:
                 v.tensor_tensor(
                     out=qa[:, 0:cols].rearrange("p (b w) -> p b w", b=nblk),
                     in0=wv_.rearrange("p (b w) -> p b w", b=nblk),
                     in1=qr[:, 0:nblk].broadcast_to([128, nblk, 32]),
                     op=OP.mult))
            v_em(lambda v, cols=cols, qa=qa: v.tensor_scalar(
                out=qa[:, 0:cols], in0=qa[:, 0:cols],
                scalar1=CM, scalar2=CM, op0=OP.add, op1=OP.subtract))
            lastv = v_em(lambda v, cols=cols, nblk=nblk, qa=qa, qw=qw,
                         qs=qs: v.tensor_tensor(
                             out=qw[:, 0:cols].rearrange(
                                 "p (b w) -> p b w", b=nblk),
                             in0=qa[:, 0:cols].rearrange(
                                 "p (b w) -> p b w", b=nblk),
                             in1=qs[:, 0:nblk].broadcast_to([128, nblk, 32]),
                             op=OP.mult))
            wait("tensor", "ve", lastv)
            wait("scalar", last_d[b], None)
            for cc_ in range(cols // 128):
                tt_ = t_em(lambda t, cc_=cc_, qw=qw: t.transpose(
                    p_a[:, 0:128], qw[:, cc_ * 128:(cc_ + 1) * 128],
                    ident[:]))
                wait("scalar", "te", tt_)
                last_se = s_em(lambda s, cc_=cc_, qb=qb: s.activation(
                    out=qb[:, cc_ * 128:(cc_ + 1) * 128],
                    in_=p_a[:, 0:128], func=ACT.Identity, bias=0.0,
                    scale=1.0))
                if cc_ + 1 < cols // 128:
                    wait("tensor", "se", last_se)
            t_tr[ch] = CNT["te"].v
            wait("sync", "se", last_se)
            for cc_ in range(cols // 128):
                if nch > 1:
                    dst = d_ag_in[cc_ * 128:(cc_ + 1) * 128,
                                  ch * 128:(ch + 1) * 128]
                else:
                    dst = d_ag_in[cc_ * 128:(cc_ + 1) * 128, 0:128]
                last_d[b] = dma("sync", dst,
                                qb[:, cc_ * 128:(cc_ + 1) * 128],
                                lane="P%d" % b, new_group=(cc_ == 0))
        return (last_d[0], last_d[1])

    t_dq_u = dequant_weight(d_wup, d_agu_in.ap(), FF // NC, D)
    t_dq_g = dequant_weight(d_wgt, d_agg_in.ap(), FF // NC, D)
    t_dq_d = dequant_weight(d_wdn, d_agd_in.ap(), D, FF // NC)

    for tk in t_dq_u + t_dq_g + t_dq_d:
        wait("gpsimd", tk, None)
    t_ag_u = collective("AllGather", OP.bypass, d_agu_in, d_agu_out)
    t_ag_g = collective("AllGather", OP.bypass, d_agg_in, d_agg_out)
    t_ag_d = collective("AllGather", OP.bypass, d_agd_in, d_agd_out)
    if STOP_AFTER <= 0:
        return _finish()

    s_dq.close()

    s_att = contextlib.ExitStack()
    _open_stacks.append(s_att)
    o_n = sbuf_in(s_att, "o_n", [128, NT * 512], F32)
    qkT = sbuf_in(s_att, "qkT", [128, H * T], BF16)
    v_aug = sbuf_in(s_att, "v_aug", [128, NT * HWID], BF16)

    s_p1 = contextlib.ExitStack()
    _open_stacks.append(s_p1)
    wqk_bf = sbuf_in(s_p1, "wqk_bf", [128, NFC * D], BF16)
    wv_bf = sbuf_in(s_p1, "wv_bf", [128, NFC * CD], BF16)
    xfT_bf = sbuf_in(s_p1, "xfT_bf", [128, NFC * T], BF16)
    sqs = x2big
    bvf = x2big[:, 2048:2560]
    wstage = x2big[:, 2560:3584]

    fence(["vector"])
    wait("vector", t_xf, None)
    t_xbf = 0
    for fc in range(NFC):
        t_xbf = v_em(lambda v, fc=fc: v.tensor_copy(
            out=xfT_bf[:, fc * T:(fc + 1) * T],
            in_=xfT_f32[:, fc * T:(fc + 1) * T]))
    wait("sync", "ve", t_xbf)
    t_bvf = dma("sync", bvf, d_bv.partition_broadcast(128))

    def load_cast(dsrc, dst_bf, ncols, nch, stage, prev=0):
        last = prev
        for chn in range(nch):
            if last:
                wait("sync", "ve", last)
            tw = dma("sync", stage[:, 0:ncols],
                     dsrc[chn * 128:(chn + 1) * 128, :])
            wait("vector", tw, None)
            last = v_em(lambda v, chn=chn, ncols=ncols, dst_bf=dst_bf,
                        stage=stage: v.tensor_copy(
                            out=dst_bf[:, chn * ncols:(chn + 1) * ncols],
                            in_=stage[:, 0:ncols]))
        return last

    wait("sync", "ve", t_xbf)

    t_wqk = load_cast(d_wqk, wqk_bf, D, NFC, wstage)
    t_wv = load_cast(d_wv, wv_bf, CD, NFC, wstage, prev=t_wqk)

    if DEBUG:
        wait("gpsimd", "ve", t_wqk)
        dma("gpsimd", dbg["wqkbf"], wqk_bf[:])
        dma("gpsimd", dbg["xfbf"], xfT_bf[:])
        wait("tensor", "G", CNT["G"].v)
    wait("tensor", "ve", t_wqk)
    wait("scalar", t_bqk, None)
    se_qk = {}
    for kc in range(NFC):
        pb = p_a if kc % 2 == 0 else p_b
        if kc >= 2:
            wait("tensor", "se", se_qk[kc - 2])
        for fc in range(NFC):
            for tw in range(4):
                def fn(t, kc=kc, tw=tw, fc=fc, pb=pb):
                    return t.matmul(
                        pb[:, tw * 512:(tw + 1) * 512],
                        wqk_bf[:, fc * D + kc * 128:fc * D + (kc + 1) * 128],
                        xfT_bf[:, fc * T + tw * 512:fc * T + (tw + 1) * 512],
                        start=(fc == 0), stop=(fc == NFC - 1))
                t_em(fn)
        tmm = CNT["te"].v
        wait("scalar", "te", tmm)
        se_qk[kc] = s_em(lambda s, kc=kc, pb=pb: s.activation(
            out=qkT[:, kc * T:(kc + 1) * T], in_=pb[:, 0:T], func=ACT.Identity,
            bias=bqk_sb[:, kc:kc + 1], scale=1.0))

    t_vone = v_em(lambda v: v.memset(v_aug[:], 1.0))
    wait("tensor", "ve", max(t_wv, t_vone))
    wait("tensor", "se", se_qk[NFC - 1])
    wait("vector", t_bvf, None)
    ve_v = {}
    for tt in range(NT):
        pb = p_a if tt % 2 == 0 else p_b
        if tt >= 2:
            wait("tensor", "ve", ve_v[tt - 2])
        for fc in range(NFC):
            def fn(t, tt=tt, fc=fc, pb=pb):
                return t.matmul(
                    pb[:, 0:512],
                    xfT_bf[:, fc * T + tt * 128:fc * T + (tt + 1) * 128],
                    wv_bf[:, fc * CD:(fc + 1) * CD],
                    start=(fc == 0), stop=(fc == NFC - 1))
            t_em(fn)
        tmm = CNT["te"].v
        wait("vector", "te", tmm)
        ve_v[tt] = v_em(lambda v, tt=tt, pb=pb: v.tensor_tensor(
            out=v_aug[:, tt * HWID:(tt + 1) * HWID].rearrange(
                "p (h w) -> p h w", h=H)[:, :, 0:DH],
            in0=pb[:, 0:512].rearrange("p (h w) -> p h w", h=H),
            in1=bvf.rearrange("p (h w) -> p h w", h=H),
            op=OP.add))

    wait("tensor", "ve", ve_v[NT - 1])
    for kc in range(NFC):
        if kc > 0:
            wait("vector", "te", CNT["te"].v)
        tv = v_em(lambda v, kc=kc: v.tensor_tensor(
            out=sqs[:, 0:T], in0=qkT[:, kc * T:(kc + 1) * T],
            in1=qkT[:, kc * T:(kc + 1) * T], op=OP.mult))
        wait("tensor", "ve", max(tv, last_emat))
        for iw in range(4):
            def fn(t, kc=kc, iw=iw):
                return t.matmul(p_a[0:16, iw * 512:(iw + 1) * 512],
                                emat[:, kc * 16:(kc + 1) * 16],
                                sqs[:, iw * 512:(iw + 1) * 512],
                                start=(kc == 0), stop=(kc == NFC - 1))
            t_em(fn)
    t_nmm = CNT["te"].v
    wait("vector", "te", t_nmm)
    t_nred = v_em(lambda v: v.tensor_reduce(out=nrm[:, 0:1], in_=p_a[0:16, :],
                                            axis=AX, op=OP.max))
    wait("tensor", "ve", t_nred)
    t_ntr = t_em(lambda t: t.transpose(p_b[0:1, 0:16], nrm[0:16, 0:1],
                                       ident[0:16, 0:16]))
    wait("vector", "te", t_ntr)
    t_rowcopy = v_em(lambda v: v.tensor_copy(out=row[0:1, 0:16],
                                             in_=p_b[0:1, 0:16]))
    v_em(lambda v: v.tensor_tensor(out=row[0:1, 16:24], in0=row[0:1, 0:8],
                                   in1=row[0:1, 8:16], op=OP.mult))
    tv = v_em(lambda v: v.tensor_scalar(out=row[0:1, 16:24],
                                        in0=row[0:1, 16:24],
                                        scalar1=0.015625, scalar2=None,
                                        op0=OP.mult))
    wait("scalar", "ve", tv)
    t_sq = s_em(lambda s: s.activation(out=row[0:1, 16:24],
                                       in_=row[0:1, 16:24],
                                       func=ACT.Sqrt, bias=0.0, scale=1.0))
    wait("vector", "se", t_sq)
    t_negc = v_em(lambda v: v.tensor_scalar(out=row[0:1, 24:32],
                                            in0=row[0:1, 16:24],
                                            scalar1=-1.0, scalar2=-1.0,
                                            op0=OP.mult, op1=OP.add))
    wait("gpsimd", "ve", t_negc)
    t_ng = dma("gpsimd", d_scal[16:24], row[0:1, 24:32])
    wait("gpsimd", t_ng, None)
    t_ng2 = dma("gpsimd", negcp[:], d_scal[16:24].partition_broadcast(128))
    if STOP_AFTER <= 1:
        return _finish()

    if DEBUG:
        dma("gpsimd", dbg["qkT"], qkT[:])
        dma("gpsimd", dbg["vaug"], v_aug[:])
        dma("gpsimd", dbg["row"], row[:])

    s_p1.close()

    s_p2 = contextlib.ExitStack()
    _open_stacks.append(s_p2)
    ptil = [sbuf_in(s_p2, f"ptil{i}", [128, 1024], BF16) for i in range(4)]
    pvscr = sbuf_in(s_p2, "pvscr", [65, 2 * 2048], F32)

    wait("scalar", t_ng2, None)
    fence(["scalar", "vector"])
    wait("tensor", "ve", t_rowcopy)
    warmup(24, qkT[0:64, 0:128], qkT[0:64, 0:512], p_a[:, 0:512])
    se_exp = None
    exp_hist = {}
    pv_hist = {}
    for hp in range(4):
        qc = 2 * hp
        kcc = 2 * hp + 1
        for ih in range(2):
            it = hp * 2 + ih
            units = [(j, head) for j in range(NT) for head in range(2)]

            def emit_qk(u):
                j, head = units[u]
                uu = (it * NT + j) * 2 + head
                base = head * 64
                pq = (uu % 2) * 1024
                if uu >= 2:
                    wait("tensor", "se", exp_hist[uu - 2])
                for iw in range(2):
                    def fn(t, j=j, iw=iw, base=base, pq=pq, qc=qc, kcc=kcc,
                           ih=ih):
                        return t.matmul(
                            p_a[:, pq + iw * 512:pq + (iw + 1) * 512],
                            qkT[base:base + 64,
                                kcc * T + j * 128:kcc * T + (j + 1) * 128],
                            qkT[base:base + 64,
                                qc * T + ih * 1024 + iw * 512:
                                qc * T + ih * 1024 + (iw + 1) * 512],
                            start=True, stop=True, tile_position=(base, 0))
                    t_em(fn)
                t_qk = CNT["te"].v
                wait("scalar", "te", t_qk)
                if uu >= 4:
                    wait("scalar", "te", pv_hist[uu - 4])
                hg = 2 * hp + head
                exp_hist[uu] = s_em(lambda s, hg=hg, uu=uu, pq=pq:
                                    s.activation(
                    out=ptil[uu % 4][:], in_=p_a[:, pq:pq + 1024],
                    func=ACT.Exp, bias=negcp[:, hg:hg + 1], scale=0.125),
                    dep=False)

            def emit_pv(u):
                j, head = units[u]
                uu = (it * NT + j) * 2 + head
                hg = 2 * hp + head
                wait("tensor", "se", exp_hist[uu])
                for iw in range(2):
                    def fn(t, j=j, head=head, iw=iw, hg=hg, uu=uu):
                        return t.matmul(
                            p_b[0:65, (head * 2 + iw) * 512:
                                (head * 2 + iw + 1) * 512],
                            v_aug[:, j * HWID + hg * 65:
                                  j * HWID + (hg + 1) * 65],
                            ptil[uu % 4][:, iw * 512:(iw + 1) * 512],
                            start=(j == 0), stop=(j == NT - 1))
                    pv_hist[uu] = t_em(fn)

            if PIPE_ATT:
                emit_qk(0)
                emit_qk(1)
                for u in range(2, 32):
                    emit_qk(u)
                    emit_pv(u - 2)
                emit_pv(30)
                emit_pv(31)
            else:
                for u in range(32):
                    emit_qk(u)
                    emit_pv(u)
            se_exp = exp_hist[(it * NT + NT - 1) * 2 + 1]
            t_pvmm = CNT["te"].v
            wait("vector", "te", t_pvmm)
            ve_pvev = v_em(lambda v, it=it: v.tensor_copy(
                out=pvscr[:, (it % 2) * 2048:(it % 2) * 2048 + 2048],
                in_=p_b[0:65, :]))
            wait("tensor", "ve", ve_pvev)
            for head in range(2):
                hg = 2 * hp + head
                for iw in range(2):
                    for q4 in range(4):
                        def fn(t, it=it, head=head, iw=iw, q4=q4):
                            src = pvscr[:, (it % 2) * 2048
                                        + (head * 2 + iw) * 512 + q4 * 128:
                                        (it % 2) * 2048
                                        + (head * 2 + iw) * 512 + q4 * 128
                                        + 128]
                            return t.transpose(
                                p_b[:, q4 * 512:q4 * 512 + 65], src,
                                ident[0:65, 0:65])
                        t_em(fn)
                    t_tr4 = CNT["te"].v
                    wait("vector", "te", t_tr4)
                    vrec = v_em(lambda v: v.reciprocal(
                        out=mu[:, 0:4],
                        in_=p_b[:, 0:2048].rearrange(
                            "p (f c) -> p f c", c=512)[:, :, 64]))
                    last_on = 0
                    for q4 in range(4):
                        itile = ih * 8 + iw * 4 + q4
                        def fn(v, q4=q4, itile=itile, hg=hg):
                            return v.tensor_scalar(
                                out=o_n[:, itile * 512 + hg * 64:
                                        itile * 512 + hg * 64 + 64],
                                in0=p_b[:, q4 * 512:q4 * 512 + 64],
                                scalar1=mu[:, q4:q4 + 1], scalar2=None,
                                op0=OP.mult)
                        last_on = v_em(fn, dep=(q4 == 0))
                    wait("tensor", "ve", last_on)

    se_exp_last = se_exp
    if STOP_AFTER <= 2:
        return _finish()
    if DEBUG:
        tv0 = CNT["ve"].v
        wait("gpsimd", "ve", tv0)
        dma("gpsimd", dbg["on"], o_n[:])
    s_p2.close()

    s_p3 = contextlib.ExitStack()
    _open_stacks.append(s_p3)
    wout_bf = sbuf_in(s_p3, "wout_bf", [128, 4 * D], BF16)
    o_nT = sbuf_in(s_p3, "o_nT", [128, 4 * T], BF16)
    xtm = [sbuf_in(s_p3, "xtm0", [128, D], F32),
           sbuf_in(s_p3, "xtm1", [128, D], F32)]
    vt1 = sbuf_in(s_p3, "vt1", [128, D], F32)
    statb = sbuf_in(s_p3, "statb", [128, NT * 12], F32)
    mub = sbuf_in(s_p3, "mub", [128, 6 * NT], F32)
    ut1 = sbuf_in(s_p3, "ut1", [128, D], F32)
    boutf = sbuf_in(s_p3, "boutf", [128, D], F32)
    g1f = sbuf_in(s_p3, "g1f", [128, D], F32)
    be1f = sbuf_in(s_p3, "be1f", [128, D], F32)
    wstg2 = sbuf_in(s_p3, "wstg2", [128, D], F32)

    fence(["sync", "vector", "scalar"])
    t_boutf = dma("sync", boutf[:], d_bout.partition_broadcast(128))
    t_g1 = dma("sync", g1f[:], d_g1.partition_broadcast(128))
    t_be1 = dma("sync", be1f[:], d_be1.partition_broadcast(128))
    t_wout = load_cast(d_wout, wout_bf, D, 4, wstg2)
    wait("vector", t_be1, None)
    v_em(lambda v: v.tensor_scalar(out=g1f[:], in0=g1f[:], scalar1=R2,
                                   scalar2=None, op0=OP.mult))
    t_gb = v_em(lambda v: v.tensor_scalar(out=be1f[:], in0=be1f[:],
                                          scalar1=R2, scalar2=None,
                                          op0=OP.mult))

    if STOP_AFTER <= 2.3:
        return _finish()
    wait("tensor", "se", se_exp_last)
    last_tse = 0
    tse_h = {}
    for idx in range(4 * NT):
        m, tt = divmod(idx, NT)
        half = (idx % 2) * 512
        if idx >= 2:
            wait("tensor", "se", tse_h[idx - 2])
        ttr = t_em(lambda t, m=m, tt=tt, half=half: t.transpose(
            p_a[:, half:half + 128],
            o_n[:, tt * 512 + m * 128:tt * 512 + (m + 1) * 128],
            ident[:]))
        wait("scalar", "te", ttr)
        tse_h[idx] = last_tse = s_em(lambda s, m=m, tt=tt, half=half:
                                     s.activation(
            out=o_nT[:, m * T + tt * 128:m * T + (tt + 1) * 128],
            in_=p_a[:, half:half + 128], func=ACT.Identity, bias=0.0,
            scale=1.0), dep=False)

    if STOP_AFTER <= 2.5:
        return _finish()
    t_xtm = {}
    for tt in range(2):
        t_xtm[tt] = dma("sync", xtm[tt % 2][:],
                        d_x_tm[tt * 128:(tt + 1) * 128, :],
                        lane="U%d" % (tt % 2))
    x2 = x2big
    wait("tensor", "ve", t_wout)
    wait("tensor", "se", last_tse)
    wait("vector", t_boutf, None)

    def batch_ln():
        g4 = statb[:].rearrange("p (t g c) -> p t g c", g=4, c=3)
        means = g4[:, :, :, 1]
        cvs = g4[:, :, :, 2]
        dev = mub[:, 2 * NT:6 * NT].rearrange("p (t g) -> p t g", g=4)
        v_em(lambda v: v.tensor_reduce(out=mub[:, 0:NT], in_=means, axis=AX,
                                       op=OP.add))
        v_em(lambda v: v.tensor_scalar(out=mub[:, 0:NT], in0=mub[:, 0:NT],
                                       scalar1=0.25, scalar2=None,
                                       op0=OP.mult))
        v_em(lambda v: v.tensor_tensor(
            out=dev, in0=means,
            in1=mub[:, 0:NT].rearrange("p (t o) -> p t o", o=1
                                       ).broadcast_to([128, NT, 4]),
            op=OP.subtract))
        v_em(lambda v: v.tensor_tensor(out=dev, in0=dev, in1=dev,
                                       op=OP.mult))
        v_em(lambda v: v.tensor_reduce(out=mub[:, NT:2 * NT], in_=dev,
                                       axis=AX, op=OP.add))
        v_em(lambda v: v.tensor_reduce(out=mub[:, 2 * NT:3 * NT], in_=cvs,
                                       axis=AX, op=OP.add))
        v_em(lambda v: v.scalar_tensor_tensor(
            out=mub[:, NT:2 * NT], in0=mub[:, NT:2 * NT], scalar=256.0,
            in1=mub[:, 2 * NT:3 * NT], op0=OP.mult, op1=OP.add))
        v_em(lambda v: v.tensor_scalar(out=mub[:, NT:2 * NT],
                                       in0=mub[:, NT:2 * NT],
                                       scalar1=1.0 / 1024.0, scalar2=EPS,
                                       op0=OP.mult, op1=OP.add))
        tv = v_em(lambda v: v.reciprocal(out=mub[:, NT:2 * NT],
                                         in_=mub[:, NT:2 * NT]))
        wait("scalar", "ve", tv)
        ts_ = s_em(lambda s: s.activation(out=mub[:, NT:2 * NT],
                                          in_=mub[:, NT:2 * NT],
                                          func=ACT.Sqrt, bias=0.0,
                                          scale=1.0))
        wait("vector", "se", ts_)

    ve_evac = {}
    for tt in range(NT):
        pb = p_a if tt % 2 == 0 else p_b
        if tt >= 2:
            wait("tensor", "ve", ve_evac[tt - 2])
        for m in range(4):
            for fw in range(2):
                def fn(t, tt=tt, m=m, fw=fw, pb=pb):
                    return t.matmul(
                        pb[:, fw * 512:(fw + 1) * 512],
                        o_nT[:, m * T + tt * 128:m * T + (tt + 1) * 128],
                        wout_bf[:, m * D + fw * 512:m * D + (fw + 1) * 512],
                        start=(m == 0), stop=(m == 3))
                t_em(fn)
        t_mm = CNT["te"].v
        wait("vector", "te", t_mm)
        v_em(lambda v, pb=pb, tt=tt: v.tensor_tensor(
            out=x2[:, tt * D:(tt + 1) * D], in0=pb[:, 0:D], in1=boutf[:],
            op=OP.add))
        v_em(lambda v, tt=tt: v.bn_stats(
            out=statb[:, tt * 12:tt * 12 + 6],
            in_=x2[:, tt * D:tt * D + 512]))
        ve_evac[tt] = v_em(lambda v, tt=tt: v.bn_stats(
            out=statb[:, tt * 12 + 6:tt * 12 + 12],
            in_=x2[:, tt * D + 512:tt * D + 1024]), dep=False)
    batch_ln()

    for tt in range(NT):
        wait("vector", t_xtm[tt], None)
        v_em(lambda v, tt=tt: v.tensor_scalar(
            out=vt1[:], in0=x2[:, tt * D:(tt + 1) * D],
            scalar1=mub[:, tt:tt + 1], scalar2=mub[:, NT + tt:NT + tt + 1],
            op0=OP.subtract, op1=OP.mult))
        v_em(lambda v: v.tensor_tensor(out=vt1[:], in0=vt1[:], in1=g1f[:],
                                       op=OP.mult))
        v_em(lambda v: v.tensor_tensor(out=vt1[:], in0=vt1[:], in1=be1f[:],
                                       op=OP.add))
        vu = v_em(lambda v, tt=tt: v.scalar_tensor_tensor(
            out=x2[:, tt * D:(tt + 1) * D], in0=xtm[tt % 2][:],
            scalar=SQ2P1, in1=vt1[:], op0=OP.mult, op1=OP.add))
        if tt + 2 < NT:
            wait("sync", "ve", vu)
            t_xtm[tt + 2] = dma("sync", xtm[tt % 2][:],
                                d_x_tm[(tt + 2) * 128:(tt + 3) * 128, :],
                                lane="U%d" % (tt % 2))
        v_em(lambda v, tt=tt: v.bn_stats(
            out=statb[:, tt * 12:tt * 12 + 6],
            in_=x2[:, tt * D:tt * D + 512]))
        v_em(lambda v, tt=tt: v.bn_stats(
            out=statb[:, tt * 12 + 6:tt * 12 + 12],
            in_=x2[:, tt * D + 512:tt * D + 1024]), dep=False)
    batch_ln()

    ve_x2 = {}
    for tt in range(NT):
        v_em(lambda v, tt=tt: v.tensor_scalar(
            out=x2[:, tt * D:(tt + 1) * D], in0=x2[:, tt * D:(tt + 1) * D],
            scalar1=mub[:, tt:tt + 1], scalar2=mub[:, NT + tt:NT + tt + 1],
            op0=OP.subtract, op1=OP.mult))
        ve_x2[tt] = v_em(lambda v, tt=tt: v.tensor_reduce(
            out=xmax[:, tt:tt + 1], in_=x2[:, tt * D:(tt + 1) * D],
            axis=AX, op=OP.max, apply_absolute_value=True))

    if DEBUG:
        tv0 = CNT["ve"].v
        wait("gpsimd", "ve", tv0)
        dma("gpsimd", dbg["x2"], x2[:])
    s_p3.close()
    s_att.close()
    if STOP_AFTER <= 3:
        return _finish()

    v_em(lambda v: v.tensor_reduce(out=red1[:], in_=xmax[:], axis=AX,
                                   op=OP.max))
    cross_part_max(red1[:], 0)
    allreduce_row(0, 1)
    v_em(lambda v: v.tensor_scalar(out=row[0:1, 1:2], in0=ar_sb[0:1, 0:1],
                                   scalar1=1.0 / 32767.0, scalar2=1e-12,
                                   op0=OP.mult, op1=OP.max))
    v_em(lambda v: v.reciprocal(out=row[0:1, 2:3], in_=row[0:1, 1:2]))
    t_scb = bcast_scal(1, 2, 1)
    if STOP_AFTER <= 4:
        return _finish()

    s_ffn = contextlib.ExitStack()
    _open_stacks.append(s_ffn)
    ftile = [sbuf_in(s_ffn, "ft0", [128, T], F32),
             sbuf_in(s_ffn, "ft1", [128, T], F32)]
    fA = [sbuf_in(s_ffn, "fA0", [128, T], F32),
          sbuf_in(s_ffn, "fA1", [128, T], F32)]

    s_f13 = contextlib.ExitStack()
    _open_stacks.append(s_f13)
    xqT = sbuf_in(s_f13, "xqT", [128, NFC * T], BF16)
    aq = sbuf_in(s_f13, "aq", [128, 256], F32)
    wst = [sbuf_in(s_f13, "wst0", [128, NFC * 128], BF16),
           sbuf_in(s_f13, "wst1", [128, NFC * 128], BF16)]
    fstage = [sbuf_in(s_f13, "fst0", [128, T], F32),
              sbuf_in(s_f13, "fst1", [128, T], F32)]
    sgt = sbuf_in(s_f13, "sgt", [128, T], F32)
    qs16 = sbuf_in(s_f13, "qs16", [128, T], U16)
    ppt = [sbuf_in(s_f13, "pp0", [128, T], F32),
           sbuf_in(s_f13, "pp1", [128, T], F32)]
    ut16 = sbuf_in(s_f13, "ut16", [128, T], F32)

    def wld_src(k):
        wi, oc = divmod(k, NOC)
        d_ag = d_agu_out if wi == 0 else d_agg_out
        rank, lo = oc // 3, (oc % 3) * 128
        return d_ag.ap()[rank, :, lo:lo + 128].rearrange("(c p) o -> p c o",
                                                         p=128)

    t_wld = {}
    fence(["sync"])
    wait("sync", "cc", t_ag_g)
    for k in range(2):
        t_wld[k] = dma("sync", wst[k % 2][:].rearrange("p (c o) -> p c o",
                                                       c=NFC), wld_src(k),
                       lane="W%d" % (k % 2))

    fence(["vector"])
    wait("tensor", t_scb, None)
    wait("tensor", "ve", ve_x2[NT - 1])
    tvq = {}
    for idx in range(NFC * NT):
        fc, tt = divmod(idx, NT)
        half = (idx % 2) * 512
        ah = (idx % 2) * 128
        if idx >= 2:
            wait("tensor", "ve", tvq[idx - 2])
        ttr = t_em(lambda t, fc=fc, tt=tt, half=half: t.transpose(
            p_a[:, half:half + 128],
            x2[:, tt * D + fc * 128:tt * D + (fc + 1) * 128], ident[:]))
        wait("vector", "te", ttr)
        v_em(lambda v, half=half, ah=ah: v.tensor_scalar(
            out=aq[:, ah:ah + 128], in0=p_a[:, half:half + 128],
            scalar1=scal_bc[:, 2:3], scalar2=CM,
            op0=OP.mult, op1=OP.add), dep=False)
        tvq[idx] = v_em(lambda v, fc=fc, tt=tt, ah=ah: v.tensor_scalar(
            out=xqT[:, fc * T + tt * 128:fc * T + (tt + 1) * 128],
            in0=aq[:, ah:ah + 128], scalar1=CM, scalar2=scal_bc[:, 1:2],
            op0=OP.subtract, op1=OP.mult))
    t_xq = CNT["ve"].v
    if STOP_AFTER <= 5:
        return _finish()
    if DEBUG:
        wait("gpsimd", "ve", t_xq)
        dma("gpsimd", dbg["xq"], xqT[:])

    wait("tensor", "ve", t_xq)
    te_f1 = {}
    se_fev = {}
    t_fsp = {}
    ve_fred = {}

    for k in range(2 * NOC):
        wi, oc = divmod(k, NOC)
        d_raw = d_upraw if wi == 0 else d_gtraw
        pb = p_a if k % 2 == 0 else p_b
        wait("tensor", t_wld[k], None)
        if k >= 2:
            wait("tensor", "se", se_fev[k - 2])
        for hc in range(NFC):
            for tw in range(4):
                def fn(t, k=k, tw=tw, hc=hc, pb=pb):
                    return t.matmul(
                        pb[:, tw * 512:(tw + 1) * 512],
                        wst[k % 2][:, hc * 128:(hc + 1) * 128],
                        xqT[:, hc * T + tw * 512:hc * T + (tw + 1) * 512],
                        start=(hc == 0), stop=(hc == NFC - 1))
                t_em(fn)
        te_f1[k] = CNT["te"].v
        if k + 2 < 2 * NOC:
            wait("sync", "te", te_f1[k])
            t_wld[k + 2] = dma("sync",
                               wst[k % 2][:].rearrange("p (c o) -> p c o",
                                                       c=NFC),
                               wld_src(k + 2), lane="W%d" % (k % 2))
        wait("scalar", "te", te_f1[k])
        if k >= 2:
            wait("scalar", t_fsp[k - 2], None)
            wait("scalar", "ve", ve_fred[k - 2])
        se_fev[k] = s_em(lambda s, k=k, pb=pb: s.activation(
            out=fstage[k % 2][:], in_=pb[:, 0:T], func=ACT.Identity,
            bias=0.0, scale=1.0), dep=False)
        wait("vector", "se", se_fev[k])
        ve_fred[k] = v_em(lambda v, k=k: v.tensor_reduce(
            out=wmax[:, k:k + 1], in_=fstage[k % 2][:], axis=AX,
            op=OP.max, apply_absolute_value=True), dep=False)
        wait("sync", "se", se_fev[k])
        t_fsp[k] = dma("sync", d_raw.ap()[oc, :, :], fstage[k % 2][:],
                       lane="P%d" % (k % 2))

    zb2 = [ftile[0], ftile[1], fstage[0], fstage[1]]
    ZL = ("W0", "W1", "A", "G")
    t_f2in = {}
    for oc in range(2):
        t_f2in[oc] = dma("sync", zb2[oc][:], d_gtraw.ap()[oc, :, :],
                         lane=ZL[oc])
    wait("sync", t_fsp[2 * NOC - 1], None)
    wait("sync", t_fsp[2 * NOC - 2], None)
    wait("sync", "ve", CNT["ve"].v)
    for oc in range(2, 4):
        t_f2in[oc] = dma("sync", zb2[oc][:], d_gtraw.ap()[oc, :, :],
                         lane=ZL[oc])

    v_em(lambda v: v.tensor_reduce(out=red1[:], in_=wmax[:, 0:NOC], axis=AX,
                                   op=OP.max))
    cross_part_max(red1[:], 4)
    v_em(lambda v: v.tensor_reduce(out=red1[:], in_=wmax[:, NOC:2 * NOC],
                                   axis=AX, op=OP.max))
    cross_part_max(red1[:], 5)
    allreduce_row(4, 2)
    v_em(lambda v: v.tensor_scalar(out=row[0:1, 6:7], in0=ar_sb[0:1, 0:1],
                                   scalar1=1.0 / 32767.0, scalar2=1e-12,
                                   op0=OP.mult, op1=OP.max))
    v_em(lambda v: v.tensor_scalar(out=row[0:1, 8:9], in0=ar_sb[0:1, 1:2],
                                   scalar1=1.0 / 32767.0, scalar2=1e-12,
                                   op0=OP.mult, op1=OP.max))
    v_em(lambda v: v.reciprocal(out=row[0:1, 7:8], in_=row[0:1, 6:7]))
    v_em(lambda v: v.reciprocal(out=row[0:1, 9:10], in_=row[0:1, 8:9]))
    v_em(lambda v: v.tensor_scalar(out=row[0:1, 10:11], in0=row[0:1, 8:9],
                                   scalar1=1.0 / 65536.0, scalar2=None,
                                   op0=OP.mult))
    t_scb2 = bcast_scal(6, 5, 6)
    if STOP_AFTER <= 6:
        return _finish()

    wait("sync", t_scb2, None)
    wait("vector", t_scb2, None)
    vb2 = v_em(lambda v: v.tensor_scalar(
        out=scal_bc[:, 3:4], in0=scal_bc[:, 8:9], scalar1=-CM, scalar2=None,
        op0=OP.mult))
    t_ppo = {}
    v_pp = {}
    vqs = {}
    for oc in range(NOC):
        wait("scalar", t_f2in[oc], None)
        if oc == 0:
            wait("scalar", t_scb2, None)
            wait("scalar", "ve", vb2)
        if oc >= 2:
            wait("scalar", "ve", v_pp[oc - 2])
        sA = s_em(lambda s, oc=oc: s.activation(
            out=fA[oc % 2][:], in_=zb2[oc % 4][:], func=ACT.Identity,
            bias=cmv[:], scale=scal_bc[:, 9:10]))
        if oc + 4 < NOC:
            wait("sync", "se", sA)
            t_f2in[oc + 4] = dma("sync", zb2[oc % 4][:],
                                 d_gtraw.ap()[oc + 4, :, :],
                                 lane=ZL[oc % 4])
        if oc >= 1:
            wait("scalar", "ve", vqs[oc - 1])
        ssig = s_em(lambda s, oc=oc: s.activation(
            out=sgt[:], in_=fA[oc % 2][:], func=ACT.Sigmoid,
            bias=scal_bc[:, 3:4], scale=scal_bc[:, 8:9]))
        ssc = s_em(lambda s: s.activation(out=sgt[:], in_=sgt[:],
                                          func=ACT.Identity, bias=cmv[:],
                                          scale=65536.0))
        wait("vector", "se", ssc)
        vqs[oc] = v_em(lambda v: v.tensor_scalar(
            out=qs16[:], in0=sgt[:], scalar1=CM, scalar2=65535.0,
            op0=OP.subtract, op1=OP.min))
        if oc >= 2:
            wait("vector", t_ppo[oc - 2], None)
        v_pp[oc] = v_em(lambda v, oc=oc: v.scalar_tensor_tensor(
            out=ppt[oc % 2][:], in0=fA[oc % 2][:], scalar=CM, in1=qs16[:],
            op0=OP.subtract, op1=OP.mult))
        vr = v_em(lambda v, oc=oc: v.tensor_reduce(
            out=ppmax[:, oc:oc + 1], in_=ppt[oc % 2][:], axis=AX, op=OP.max,
            apply_absolute_value=True))
        wait("sync", "ve", vr)
        t_ppo[oc] = dma("sync", d_pp.ap()[oc, :, :], ppt[oc % 2][:],
                        lane="P%d" % (oc % 2))

    t_f3p = {}
    t_f3u = {}
    wait("sync", "se", CNT["se"].v)
    wait("sync", "ve", CNT["ve"].v)
    for oc in range(4):
        t_f3p[oc] = dma("sync", zb2[oc % 4][:], d_pp.ap()[oc, :, :],
                        lane=ZL[oc % 4])
    for oc in range(2):
        t_f3u[oc] = dma("sync", fA[oc % 2][:], d_upraw.ap()[oc, :, :],
                        lane="U%d" % (oc % 2))

    v_em(lambda v: v.tensor_reduce(out=red1[:], in_=ppmax[:], axis=AX,
                                   op=OP.max))
    cross_part_max(red1[:], 11)
    v_em(lambda v: v.tensor_tensor(out=row[0:1, 11:12], in0=row[0:1, 11:12],
                                   in1=row[0:1, 10:11], op=OP.mult))
    allreduce_row(11, 1)
    v_em(lambda v: v.tensor_scalar(out=row[0:1, 12:13], in0=ar_sb[0:1, 0:1],
                                   scalar1=1.0 / 32767.0, scalar2=1e-12,
                                   op0=OP.mult, op1=OP.max))
    v_em(lambda v: v.reciprocal(out=row[0:1, 13:14], in_=row[0:1, 12:13]))
    v_em(lambda v: v.tensor_tensor(out=row[0:1, 13:14], in0=row[0:1, 13:14],
                                   in1=row[0:1, 10:11], op=OP.mult))
    t_scb3 = bcast_scal(12, 2, 12)
    if STOP_AFTER <= 7:
        return _finish()

    wait("sync", t_scb3, None)
    t_zzo = {}
    v_zz = {}
    for oc in range(NOC):
        wait("scalar", t_f3p[oc], None)
        wait("scalar", t_f3u[oc], None)
        if oc == 0:
            wait("scalar", t_scb3, None)
        if oc >= 2:
            wait("scalar", "ve", v_zz[oc - 2])
        sA = s_em(lambda s, oc=oc: s.activation(
            out=zb2[oc % 4][:], in_=zb2[oc % 4][:], func=ACT.Identity,
            bias=cmv[:], scale=scal_bc[:, 13:14]))
        sB = s_em(lambda s, oc=oc: s.activation(
            out=fA[oc % 2][:], in_=fA[oc % 2][:], func=ACT.Identity,
            bias=cmv[:], scale=scal_bc[:, 7:8]))
        if oc >= 1:
            wait("scalar", "ve", v_zz[oc - 1])
        s_ut = s_em(lambda s, oc=oc: s.activation(
            out=ut16[:], in_=fA[oc % 2][:], func=ACT.Identity,
            bias=ncmv[:], scale=1.0))
        wait("vector", "se", s_ut)
        if oc >= 2:
            wait("vector", t_zzo[oc - 2], None)
        else:
            wait("vector", t_ppo[NOC - 2 + oc], None)
        v_zz[oc] = v_em(lambda v, oc=oc: v.scalar_tensor_tensor(
            out=ppt[oc % 2][:], in0=zb2[oc % 4][:], scalar=CM, in1=ut16[:],
            op0=OP.subtract, op1=OP.mult))
        vr = v_em(lambda v, oc=oc: v.tensor_reduce(
            out=zzmax[:, oc:oc + 1], in_=ppt[oc % 2][:], axis=AX, op=OP.max,
            apply_absolute_value=True))
        if oc + 4 < NOC:
            wait("sync", "ve", vr)
            t_f3p[oc + 4] = dma("sync", zb2[oc % 4][:],
                                d_pp.ap()[oc + 4, :, :],
                                lane=ZL[oc % 4])
        if oc + 2 < NOC:
            wait("sync", "ve", vr)
            t_f3u[oc + 2] = dma("sync", fA[oc % 2][:],
                                d_upraw.ap()[oc + 2, :, :],
                                lane="U%d" % (oc % 2))
        wait("sync", "ve", vr)
        t_zzo[oc] = dma("sync", d_zz.ap()[oc, :, :], ppt[oc % 2][:],
                        lane="P%d" % (oc % 2))

    t_f4in = {}
    wait("sync", "se", CNT["se"].v)
    wait("sync", "ve", CNT["ve"].v)
    for oc in range(2):
        t_f4in[oc] = dma("sync", ftile[oc % 2][:], d_zz.ap()[oc, :, :],
                         lane="W%d" % (oc % 2))

    v_em(lambda v: v.tensor_reduce(out=red1[:], in_=zzmax[:], axis=AX,
                                   op=OP.max))
    cross_part_max(red1[:], 14)
    v_em(lambda v: v.tensor_tensor(out=row[0:1, 14:15], in0=row[0:1, 14:15],
                                   in1=row[0:1, 12:13], op=OP.mult))
    v_em(lambda v: v.tensor_tensor(out=row[0:1, 14:15], in0=row[0:1, 14:15],
                                   in1=row[0:1, 6:7], op=OP.mult))
    allreduce_row(14, 1)
    v_em(lambda v: v.tensor_scalar(out=row[0:1, 14:15], in0=ar_sb[0:1, 0:1],
                                   scalar1=1.0 / 32767.0, scalar2=1e-12,
                                   op0=OP.mult, op1=OP.max))
    v_em(lambda v: v.reciprocal(out=row[0:1, 15:16], in_=row[0:1, 14:15]))
    v_em(lambda v: v.tensor_tensor(out=row[0:1, 15:16], in0=row[0:1, 15:16],
                                   in1=row[0:1, 12:13], op=OP.mult))
    v_em(lambda v: v.tensor_tensor(out=row[0:1, 15:16], in0=row[0:1, 15:16],
                                   in1=row[0:1, 6:7], op=OP.mult))
    t_scb4 = bcast_scal(14, 2, 14)
    if STOP_AFTER <= 8:
        return _finish()

    s_f13.close()

    s_f4 = contextlib.ExitStack()
    _open_stacks.append(s_f4)
    qz = sbuf_in(s_f4, "qz", [128, NOC * T], BF16)
    wdnt = [sbuf_in(s_f4, "wdn%d" % i, [128, D], BF16)
            for i in range(4)]
    ot = sbuf_in(s_f4, "ot", [128, D], F32)

    fence(["vector", "sync", "scalar"])
    wait("scalar", t_scb4, None)
    t_qz = 0
    vz4 = {}
    for oc in range(NOC):
        wait("scalar", t_f4in[oc], None)
        if oc >= 2:
            wait("scalar", "ve", vz4[oc - 2])
        sA = s_em(lambda s, oc=oc: s.activation(
            out=fA[oc % 2][:], in_=ftile[oc % 2][:], func=ACT.Identity,
            bias=cmv[:], scale=scal_bc[:, 15:16]))
        if oc + 2 < NOC:
            wait("sync", "se", sA)
            t_f4in[oc + 2] = dma("sync", ftile[oc % 2][:],
                                 d_zz.ap()[oc + 2, :, :],
                                 lane="W%d" % (oc % 2))
        wait("vector", "se", sA)
        vz4[oc] = t_qz = v_em(lambda v, oc=oc: v.tensor_scalar(
            out=qz[:, oc * T:(oc + 1) * T], in0=fA[oc % 2][:],
            scalar1=CM, scalar2=None, op0=OP.subtract))

    wait("sync", "cc", t_ag_d)

    def load_wdn(oc):
        return dma("sync", wdnt[oc % 4][:],
                   d_agd_out.ap()[oc // 3,
                                  (oc % 3) * 128:(oc % 3) * 128 + 128, :],
                   lane=("U0", "U1", "P0", "P1")[oc % 4])

    def otb(tsub):
        return ftile[tsub // 2][:, (tsub % 2) * D:(tsub % 2 + 1) * D]

    t_yst = {}
    t_wl = {oc: load_wdn(oc) for oc in range(4)}
    se4_prev = 0
    for grp in range(4):
        te_f4 = {}
        if grp == 0:
            warmup(24, ident[:], ident[:, 0:128], p_b[:, 1536:1664])
        for oc in range(NOC):
            wait("tensor", t_wl[oc], None)
            if oc == 0:
                wait("tensor", "ve", se4_prev)
            if grp == 0:
                wait("tensor", "ve", vz4[oc])
            elif oc == 0:
                wait("tensor", "ve", t_qz)
            for tsub in range(4):
                tt = grp * 4 + tsub
                for fw in range(2):
                    def fn(t, oc=oc, tt=tt, tsub=tsub, fw=fw):
                        pb = p_a if tsub < 2 else p_b
                        off = (tsub % 2) * 1024 + fw * 512
                        return t.matmul(
                            pb[:, off:off + 512],
                            qz[:, oc * T + tt * 128:oc * T + (tt + 1) * 128],
                            wdnt[oc % 4][:, fw * 512:(fw + 1) * 512],
                            start=(oc == 0), stop=(oc == NOC - 1))
                    te_f4[oc] = t_em(fn)
            if oc + 4 < NOC:
                wait("sync", "te", te_f4[oc])
                t_wl[oc + 4] = load_wdn(oc + 4)
        t_dmm = te_f4[NOC - 1]
        sev4 = {}
        for tsub in range(4):
            pb = p_a if tsub < 2 else p_b
            off = (tsub % 2) * 1024
            tt = grp * 4 + tsub
            wait("vector", "te", t_dmm)
            if grp > 0:
                wait("vector", t_yst[(grp - 1) * 4 + tsub], None)
            sev4[tsub] = v_em(lambda v, pb=pb, off=off, tsub=tsub, tt=tt:
                              v.scalar_tensor_tensor(
                out=otb(tsub), in0=pb[:, off:off + D],
                scalar=scal_bc[:, 14:15], in1=x2[:, tt * D:(tt + 1) * D],
                op0=OP.mult, op1=OP.add))
        se4_prev = sev4[3]
        if grp < 3:
            wait("sync", "te", t_dmm)
            for oc in range(4):
                t_wl[oc] = load_wdn(oc)
        for tsub in range(4):
            tt = grp * 4 + tsub
            v_em(lambda v, tsub=tsub: v.bn_stats(
                out=stat[:, 0:6], in_=otb(tsub)[:, 0:512]))
            v_em(lambda v, tsub=tsub: v.bn_stats(
                out=stat[:, 6:12], in_=otb(tsub)[:, 512:1024]), dep=False)
            ln_stats()
            vf = v_em(lambda v, tsub=tsub: v.tensor_scalar(
                out=otb(tsub), in0=otb(tsub), scalar1=mu[:, 0:1],
                scalar2=mu[:, 1:2], op0=OP.subtract, op1=OP.mult))
            wait("sync", "ve", vf)
            t_yst[tt] = dma("sync", d_y[tt * 128:(tt + 1) * 128, :],
                            otb(tsub), lane="A")

    s_f4.close()
    s_ffn.close()

    if DEBUG:
        for _ln in LANES:
            wait("gpsimd", _ln, CNT[_ln].v)
        for nm, dsrc in [("up", d_upraw), ("gt", d_gtraw), ("pp", d_pp),
                         ("zz", d_zz)]:
            for oc in range(NOC):
                dma("gpsimd", dbg[nm][oc, :, :], dsrc.ap()[oc, :, :])
        dma("gpsimd", dbg["scal"], d_scal[0:32])
        wait("sync", "G", CNT["G"].v)

    return _finish()


def kernel(**inputs):
    x = np.asarray(inputs["L0_attn_x"], dtype=np.float32)
    W_qk = np.asarray(inputs["L0_attn_W_qk"], dtype=np.float32)
    b_qk = np.asarray(inputs["L0_attn_b_qk"], dtype=np.float32)
    W_v = np.asarray(inputs["L0_attn_W_v"], dtype=np.float32)
    b_v = np.asarray(inputs["L0_attn_b_v"], dtype=np.float32)
    W_out = np.asarray(inputs["L0_attn_W_out"], dtype=np.float32)
    b_out = np.asarray(inputs["L0_attn_b_out"], dtype=np.float32)
    g1 = np.asarray(inputs["L0_attn_ln_gamma"], dtype=np.float32)
    be1 = np.asarray(inputs["L0_attn_ln_beta"], dtype=np.float32)
    w_up = np.asarray(inputs["L0_ffn_w_up"], dtype=np.float32)
    w_gate = np.asarray(inputs["L0_ffn_w_gate"], dtype=np.float32)
    w_down = np.asarray(inputs["L0_ffn_w_down"], dtype=np.float32)

    perm = []
    for g in range(4):
        for h in (2 * g, 2 * g + 1):
            perm += [h * 128 + 2 * d for d in range(64)]
        for h in (2 * g, 2 * g + 1):
            perm += [h * 128 + 2 * d + 1 for d in range(64)]
    perm = np.asarray(perm)
    Wqk_r = np.ascontiguousarray(W_qk[:, perm])
    bqk_r = np.ascontiguousarray(b_qk[perm])

    nc = build_nc()

    in_maps = []
    for c in range(NC):
        xc = np.ascontiguousarray(x[c])
        in_maps.append({
            "x_tm": xc,
            "x_fT": np.ascontiguousarray(xc.T),
            "wqk_r": Wqk_r,
            "bqk_r": bqk_r,
            "wv": W_v,
            "bv": b_v,
            "wout": W_out,
            "bout": b_out,
            "g1": g1,
            "be1": be1,
            "wup_sl": np.ascontiguousarray(
                w_up[c * (FF // NC):(c + 1) * (FF // NC)]),
            "wgt_sl": np.ascontiguousarray(
                w_gate[c * (FF // NC):(c + 1) * (FF // NC)]),
            "wdn_sl": np.ascontiguousarray(
                w_down[:, c * (FF // NC):(c + 1) * (FF // NC)]),
        })
    res = run_bass_kernel_spmd(nc, in_maps, core_ids=list(range(NC)))
    return np.stack([np.asarray(r["y_out"]) for r in res.results], axis=0)
